# version 27
# baseline (speedup 1.0000x reference)
import numpy as np
import ml_dtypes
import concourse.bass as bass
import concourse.mybir as mybir
from concourse.bass_utils import run_bass_kernel_spmd
from contextlib import ExitStack

F32 = mybir.dt.float32
F16 = mybir.dt.float16
BF16 = mybir.dt.bfloat16
AF = mybir.ActivationFunctionType
ALU = mybir.AluOpType
AX = mybir.AxisListType
AP = bass.AP

NH, HD, DM = 16, 64, 1024
NT = 20
TW = NT * 128
QT0, QT1 = 2, 18
OT0, OT1 = 3, 18
DEBUG_DUMP = False
STAGES = 4


class Prog:
    def __init__(self, nc):
        self.nc = nc
        self.qs = {k: [] for k in "tvagy"}
        self.cnt = {}

    def add(self, name):
        self.cnt[name] = 0

    def op(self, q, build, waits=(), incs=(), fence=False):
        waits = [(s, v) for (s, v) in waits if v > 0]
        if fence and self.qs[q]:
            prev = self.qs[q][-1]
            if prev[2]:
                sem, amt, cum = prev[2][0]
                waits.append((sem, cum))
            else:
                fs = "f" + q
                self.cnt[fs] += 1
                prev[2].append((fs, 1, self.cnt[fs]))
                waits.append((fs, self.cnt[fs]))
        rec = []
        for s, a in incs:
            self.cnt[s] += a
            rec.append((s, a, self.cnt[s]))
        self.qs[q].append([waits, build, rec])

    def n(self, sem):
        return self.cnt[sem]

    def replay(self, q, eng, sems):
        for waits, build, incs in self.qs[q]:
            for s, v in waits:
                eng.wait_ge(sems[s], v)
            inst = build(eng)
            for s, a, _ in incs:
                if inst is None:
                    eng.sem_inc(sems[s], a)
                else:
                    inst.then_inc(sems[s], a)


def build_nc():
    nc = bass.Bass("TRN2", target_bir_lowering=False, debug=False)
    dt = nc.dram_tensor
    x0_d = dt("x0", [TW, DM], F32, kind="ExternalInput").ap()
    wqkv_d = dt("wqkv", [DM, 3 * DM], F32, kind="ExternalInput").ap()
    wqkvb_d = dt("wqkvb", [1, 3 * DM], F32, kind="ExternalInput").ap()
    wo_d = dt("wo", [DM, DM], F32, kind="ExternalInput").ap()
    wob_d = dt("wob", [1, DM], F32, kind="ExternalInput").ap()
    wr_d = dt("wr", [DM, DM], F32, kind="ExternalInput").ap()
    wk_d = dt("wk", [DM, DM], F32, kind="ExternalInput").ap()
    wv_d = dt("wv", [DM, DM], F32, kind="ExternalInput").ap()
    mu_d = dt("mu", [128, 16], F32, kind="ExternalInput").ap()
    cos_d = dt("cosx", [128, 2 * DM], F32, kind="ExternalInput").ap()
    sin_d = dt("sinx", [128, 2 * DM], F32, kind="ExternalInput").ap()
    pm1_d = dt("pm1", [128, DM], F16, kind="ExternalInput").ap()
    gates_d = dt("gates", [128, 3, 256], BF16, kind="ExternalInput").ap()
    bsel_d = dt("bsel", [128, 1], F32, kind="ExternalInput").ap()
    nout = (OT1 - OT0 + 1) * 128
    out_d = dt("out", [nout, DM], F32, kind="ExternalOutput").ap()
    xscr = dt("xscr", [TW, DM], F32).ap()
    wqkv16 = dt("wqkv16", [DM, 3 * DM], F16).ap()
    ffn16 = dt("ffn16", [DM, 3 * DM], F16).ap()
    dbg = {}
    if DEBUG_DUMP:
        for nm, shape in (("ln_fm", [128, 8 * 528]), ("xrm", [128, 8 * 512]),
                          ("xkm", [128, 8 * 512]), ("r_sb", [128, 2 * DM]),
                          ("kpre", [128, 2 * DM]), ("kact", [128, 2 * DM]),
                          ("ln_tm", [128, 2 * DM]), ("stats", [128, 16])):
            dbg[nm] = dt("dbg_" + nm, shape, F32, kind="ExternalOutput").ap()

    st = ExitStack()
    sb = lambda n, s, d: st.enter_context(nc.sbuf_tensor(n, s, d))

    xr_sb = sb("xr_sb", [128, 8, DM], F32)
    warena = sb("warena", [128, 8, 4 * DM], F16)
    C_sb = sb("C_sb", [128, 2, DM], F16)
    S_sb = sb("S_sb", [128, 2, DM], F16)
    Cq_sb = sb("Cq_sb", [128, 2, DM], F16)
    Sq_sb = sb("Sq_sb", [128, 2, DM], F16)
    pm1_sb = sb("pm1_sb", [128, DM], F16)
    gates_sb = sb("gates_sb", [128, 3, 256], BF16)
    mu_sb = sb("mu_sb", [128, 16], F32)
    negmu = sb("negmu", [128, 16], F32)
    wqkvb_sb = sb("wqkvb_sb", [1, 3 * DM], F16)
    wob_sb = sb("wob_sb", [1, DM], F16)
    ones1 = sb("ones1", [1, 128], F16)
    bsel_sb = sb("bsel_sb", [128, 1], F32)
    kb2s = sb("kb2s", [1, 2 * NH], F32)
    kb2row = sb("kb2row", [1, NH], F16)
    nkb2b = sb("nkb2b", [128, NH], F32)
    bsq = sb("bsq", [1, DM], F32)
    ln_tm = sb("ln_tm", [128, 2, DM], F16)
    stats = sb("stats", [128, 4, 8], F32)
    qsq = sb("qsq", [128, DM], F16)
    qsq_free = {0: []}
    qsq_waw = []
    stats_free = {}
    pb = [st.enter_context(nc.psum_tensor(f"pb{i}", [128, 512], F32)) for i in range(8)]

    P = Prog(nc)
    STREAMS = (["am", "awq0", "awq1", "awq2", "aw4r", "aw4k", "aw4v", "ylb"] +
               [f"sl{i}" for i in range(8)] + [f"gl{i}" for i in range(8)] +
               ["yl0", "yl1", "yk0", "yk1", "yt0", "yt1", "yt2", "yt3", "yaf0", "yaf1",
                "gw0a", "gw0b"])
    for s in (["g", "a", "y", "t", "v", "s", "ft", "fv", "fa", "fg", "fy",
               "gm", "gw0", "gw1", "gw2", "gwo", "gq6", "gf6"] + STREAMS):
        P.add(s)

    GD = lambda o, i: (lambda g: g.dma_start(o, i))
    AD = lambda o, i: (lambda a: a.dma_start(o, i))
    YT = lambda o, i: (lambda y: y.dma_start(o, i, transpose=True))

    bank_free = {i: [] for i in range(8)}
    lnT_free = {0: [], 1: []}

    g_ffnw3 = []
    for hf, snw in ((0, "gw0a"), (1, "gw0b")):
        P.op("g", GD(warena[:, :, 512 * hf:512 * (hf + 1)],
                     AP(wr_d.tensor, 512 * hf, [[DM, 128], [128 * DM, 8], [1, 512]])),
             incs=[(snw, 16)])
    g_ffnw3.append([("gw0a", 16), ("gw0b", 16)])
    for o, i in ((wqkvb_sb[:, :], wqkvb_d), (wob_sb[:, :], wob_d), (C_sb[:, :, :], cos_d),
                 (S_sb[:, :, :], sin_d), (pm1_sb[:, :], pm1_d)):
        P.op("g", GD(o, i), incs=[("gm", 16)])
    g_misc = ("gm", P.n("gm"))
    for nmw, (i, wd) in (("gw1", (1, wk_d)), ("gw2", (2, wv_d))):
        P.op("g", GD(warena[:, :, i * DM:(i + 1) * DM],
                     wd.rearrange("(c p) d -> p c d", p=128)), incs=[(nmw, 16)])
        g_ffnw3.append((nmw, P.n(nmw)))
    g_ffnw = list(g_ffnw3)
    P.op("g", GD(warena[:, :, 3 * DM:4 * DM], wo_d.rearrange("(c p) d -> p c d", p=128)),
         incs=[("gwo", 16)])
    g_wo = ("gwo", P.n("gwo"))
    for o, i in ((gates_sb[:, :, :], gates_d), (mu_sb[:, :], mu_d), (bsel_sb[:, :], bsel_d)):
        P.op("a", AD(o, i), incs=[("am", 16)])
    a_misc = ("am", P.n("am"))

    P.op("v", lambda v: v.memset(ones1[:, :], 1.0), incs=[("v", 1)])
    P.op("v", lambda v: v.tensor_scalar(negmu[:, :], mu_sb[:, :], 1.0, None, ALU.subtract),
         waits=[a_misc], incs=[("v", 1)])
    for par in range(2):
        P.op("v", lambda v, par=par: v.tensor_tensor(S_sb[:, par, :], S_sb[:, par, :],
                                                     pm1_sb[:, :], ALU.mult),
             waits=[g_misc], incs=[("v", 1)])
    P.op("v", lambda v: v.tensor_scalar(Cq_sb[:, :, :], C_sb[:, :, :], 0.125, None, ALU.mult),
         waits=[g_misc], incs=[("v", 1)])
    P.op("v", lambda v: v.tensor_scalar(Sq_sb[:, :, :], S_sb[:, :, :], 0.125, None, ALU.mult),
         fence=True, incs=[("v", 1)])
    v_cq = P.n("v")
    P.op("a", lambda a: a.activation(bsq[:, :], wqkvb_sb[:, DM:2 * DM], AF.Square),
         waits=[g_misc, ("v", P.n("v"))], incs=[("s", 1)])
    P.op("v", lambda v: v.tensor_reduce(kb2s[:, 0:NH],
                                        bsq[:, :].rearrange("p (h d) -> p h d", h=NH),
                                        AX.X, ALU.add),
         waits=[("s", P.n("s"))], incs=[("v", 1)])
    P.op("v", lambda v: v.tensor_scalar(kb2s[:, 0:NH], kb2s[:, 0:NH], 1e-12, None, ALU.max),
         fence=True, incs=[("v", 1)])
    P.op("a", lambda a: a.activation(kb2s[:, NH:2 * NH], kb2s[:, 0:NH], AF.Ln),
         waits=[("v", P.n("v"))], incs=[("s", 1)])
    P.op("a", lambda a: a.activation(kb2s[:, NH:2 * NH], kb2s[:, NH:2 * NH], AF.Exp, scale=0.5),
         waits=[("s", P.n("s"))], incs=[("s", 1)])
    with nc.allow_low_precision(reason="stabilizer scale; any nearby value is valid"):
        P.op("v", lambda v: v.tensor_scalar(kb2row[:, :], kb2s[:, NH:2 * NH], 14.2, -1.0,
                                            ALU.add, ALU.mult),
             waits=[("s", P.n("s"))], incs=[("v", 1)])
    P.op("t", lambda t: t.matmul(pb[7][:, 0:NH], ones1[0:1, :], kb2row[0:1, :],
                                 start=True, stop=True),
         waits=[("v", P.n("v"))], incs=[("t", 1)])
    P.op("v", lambda v: v.tensor_copy(nkb2b[:, :], pb[7][:, 0:NH]),
         waits=[("t", P.n("t"))], incs=[("v", 1)])
    P.op("v", lambda v: v.tensor_scalar(nkb2b[:, :], nkb2b[:, :], 0.125, None, ALU.mult),
         fence=True, incs=[("v", 1)])
    bank_free[7] = [("v", P.n("v"))]
    v_pre = P.n("v")

    def emit_ln_stats(x_ap, slot, par, waits):
        sl = lambda c: stats[:, slot, c:c + 1]
        P.op("a", lambda a: a.activation(ln_tm[:, par, :], x_ap, AF.Copy, accum_out=sl(0)),
             waits=list(waits) + lnT_free[par] + stats_free.pop(slot, []),
             incs=[("s", 1)])
        P.op("a", lambda a: a.activation(qsq[:, :], x_ap, AF.Square, accum_out=sl(1)),
             waits=list(qsq_free[0]) + qsq_waw[:], incs=[("s", 1)])
        qsq_waw[:] = [("s", P.n("s"))]
        s_sq = P.n("s")
        P.op("v", lambda v: v.tensor_scalar(sl(2), sl(0), 1.0 / DM, None, ALU.mult),
             waits=[("s", s_sq)], incs=[("v", 1)])
        P.op("v", lambda v: v.tensor_tensor(sl(3), sl(2), sl(2), ALU.mult), fence=True,
             incs=[("v", 1)])
        P.op("v", lambda v: v.scalar_tensor_tensor(sl(4), sl(1), 1.0 / DM, sl(3),
                                                   ALU.mult, ALU.subtract),
             fence=True, incs=[("v", 1)])
        P.op("v", lambda v: v.tensor_scalar(sl(4), sl(4), 1e-5, None, ALU.add),
             fence=True, incs=[("v", 1)])
        return P.n("v")

    def emit_rstd(slots, wtok):
        lo, n = min(slots), len(slots)
        lnv = AP(stats, lo * 8 + 6, [[4 * 8, 128], [8, n], [1, 1]])
        var = AP(stats, lo * 8 + 4, [[4 * 8, 128], [8, n], [1, 1]])
        rst = AP(stats, lo * 8 + 5, [[4 * 8, 128], [8, n], [1, 1]])
        P.op("a", lambda a: a.activation(lnv, var, AF.Ln), waits=[("v", wtok)], incs=[("s", 1)])
        P.op("a", lambda a: a.activation(rst, lnv, AF.Exp, scale=-0.5),
             waits=[("s", P.n("s"))], incs=[("s", 1)])
        return P.n("s")

    def emit_ln_apply(x_ap, slot, par, stok, waits=()):
        sl = lambda c: stats[:, slot, c:c + 1]
        P.op("v", lambda v: v.tensor_scalar(ln_tm[:, par, :], x_ap, sl(2), sl(5),
                                            ALU.subtract, ALU.mult),
             waits=[("s", stok)] + list(waits) + lnT_free[par], incs=[("v", 1)])
        stats_free[slot] = [("v", P.n("v"))]
        return P.n("v")

    def emit_ln(x_ap, par, waits):
        wtok = emit_ln_stats(x_ap, par, par, waits)
        stok = emit_rstd([par], wtok)
        return emit_ln_apply(x_ap, par, par, stok)

    def emit_ffn_stage(stage, barrier, g_w, mid=None, lastW=None, slot_init=None, spill_prev=None):
        if lastW is None:
            lastW = {}
        s1 = ExitStack()
        lsb = lambda n, s, d: s1.enter_context(nc.sbuf_tensor(n + f"_s{stage}", s, d))
        ln_fm = lsb("ln_fm", [128, 8, 528], F16)
        d_fm = lsb("d_fm", [128, 8, 512], F16)
        xrm2 = lsb("xrm2", [128, 2, 8, 512], F16)
        xkm2 = lsb("xkm2", [128, 2, 8, 512], F16)
        r_sb = lsb("r_sb", [128, 2, DM], F16)
        kpre = lsb("kpre", [128, 2, DM], F16)
        kfm = lsb("kfm", [128, 2, 8, 128], F16)
        t_sb = lsb("t_sb", [128, 2, DM], F32)

        if stage == 1:
            src = x0_d
            groups = [list(range(4 * g, 4 * g + 4)) for g in range(5)]
            bnd = None
        else:
            src = xscr
            groups = [[3, 4, 5, 6], [7, 8, 9, 10], [11, 12, 13, 14], [15, 16, 17, 18]]
            bnd = 2
        all_tiles = ([bnd] if bnd is not None else []) + [t for g in groups for t in g]

        xtok = {}
        slot_free = dict(slot_init or {})
        spill_tok = {}
        kfm_free = {0: [], 1: []}
        rkt_free = {0: [], 1: []}

        qd, pfx = ("a", "sl") if stage == 1 else ("g", "gl")

        def load_x(tl):
            sn = f"{pfx}{tl % 8}"
            w = ([slot_free[tl % 8]] if tl % 8 in slot_free else list(barrier))
            if P.n(sn) > 0:
                w.append((sn, P.n(sn)))
            if spill_prev is not None and spill_prev.get(tl):
                w.append(spill_prev[tl])
            P.op(qd, (AD if qd == "a" else GD)(xr_sb[:, tl % 8, :],
                                               src[tl * 128:(tl + 1) * 128, :]),
                 waits=w, incs=[(sn, 16)])
            xtok[tl] = (sn, P.n(sn))

        nload0 = 0
        for tl in all_tiles[:8]:
            load_x(tl); nload0 += 1

        bcol = AP(ln_fm, 15, [[8 * 528, 128], [528, 8], [1, 1]])
        if bnd is not None:
            lnb = lsb("lnb", [128, 8, 128], F16)
            lt = emit_ln(xr_sb[:, bnd % 8, :], bnd % 2, [xtok[bnd]])
            P.op("y", YT(lnb[:, :, :], ln_tm[:, bnd % 2, :]), waits=[("v", lt)],
                 incs=[("ylb", 16)])
            lnT_free[bnd % 2] = [("ylb", P.n("ylb"))]
            P.op("v", lambda v: v.tensor_scalar(
                bcol, AP(lnb, 127, [[8 * 128, 128], [128, 8], [1, 1]]),
                bsel_sb[:, 0:1], None, ALU.mult),
                waits=[("ylb", P.n("ylb")), a_misc], incs=[("v", 1)])
            slot_free[bnd % 8] = ("v", P.n("v"))
        else:
            P.op("v", lambda v: v.memset(bcol, 0.0), waits=list(barrier), incs=[("v", 1)])
        bnd_tok = P.n("v")
        mix_tok = {}
        last_rk = {}

        def emit_group_ln_pair(gi, hf):
            grp = groups[gi]
            vt = 0
            for ti in (2 * hf, 2 * hf + 1):
                tl = grp[ti]
                vt = emit_ln_stats(xr_sb[:, tl % 8, :], ti, tl % 2, [xtok[tl]])
            stok = emit_rstd([2 * hf, 2 * hf + 1], vt)
            ttok = []
            for ti in (2 * hf, 2 * hf + 1):
                tl = grp[ti]
                sn = f"yl{tl % 2}"
                lt = emit_ln_apply(xr_sb[:, tl % 8, :], ti, tl % 2, stok)
                P.op("y", YT(ln_fm[:, :, 16 + 128 * ti:144 + 128 * ti], ln_tm[:, tl % 2, :]),
                     waits=[("v", lt), ("v", bnd_tok)] +
                           ([(sn, P.n(sn))] if P.n(sn) > 0 else []) +
                           ([("v", mix_tok[(gi - 1, hf)])] if (gi - 1, hf) in mix_tok else []),
                     incs=[(sn, 16)])
                lnT_free[tl % 2] = [(sn, P.n(sn))]
                ttok.append((sn, P.n(sn)))
            return ttok

        def emit_group_mix(gi, hf, ttok):
            gp = gi % 2
            o = 256 * hf
            w = list(ttok) + [a_misc] + ([("t", last_rk[gi - 2])] if gi - 2 in last_rk else [])
            P.op("v", lambda v, o=o: v.tensor_tensor(
                d_fm[:, :, o:o + 256], ln_fm[:, :, 16 + o:272 + o],
                ln_fm[:, :, 15 + o:271 + o], ALU.subtract),
                waits=w, incs=[("v", 1)])
            for c in range(8):
                P.op("v", lambda v, c=c, gp=gp, o=o: v.scalar_tensor_tensor(
                    xrm2[:, gp, c, o:o + 256], d_fm[:, c, o:o + 256], negmu[:, c:c + 1],
                    ln_fm[:, c, 16 + o:272 + o],
                    ALU.mult, ALU.add), fence=(c == 0), incs=[("v", 1)])
            for c in range(8):
                P.op("v", lambda v, c=c, gp=gp, o=o: v.scalar_tensor_tensor(
                    xkm2[:, gp, c, o:o + 256], d_fm[:, c, o:o + 256], negmu[:, 8 + c:9 + c],
                    ln_fm[:, c, 16 + o:272 + o],
                    ALU.mult, ALU.add), incs=[("v", 1)])
            mix_tok[(gi, hf)] = P.n("v")
            if hf == 1:
                P.op("v", lambda v: v.tensor_copy(
                    bcol, AP(ln_fm, 527, [[8 * 528, 128], [528, 8], [1, 1]])),
                    fence=True, incs=[("v", 1)])
            return P.n("v")

        def emit_body_rk(gi, ti):
            tl = groups[gi][ti]
            pp = tl % 2
            gp = gi % 2
            for which, xm, woff, bank0 in (("r", xrm2, 0, 0), ("k", xkm2, DM, 2)):
                gw = g_w[0] if which == "r" else g_w[1]
                for js in range(2):
                    bk = bank0 + js
                    gwj = gw[js] if isinstance(gw, list) else gw
                    for c in range(8):
                        w = ([("v", mix_tok[(gi, ti // 2)]), gwj] + bank_free[bk]) if c == 0 else []
                        P.op("t", lambda t, bk=bk, c=c, ti=ti, js=js, xm=xm, woff=woff, gp=gp:
                             t.matmul(pb[bk][:, :],
                                      xm[:, gp, c, 128 * ti:128 * (ti + 1)],
                                      warena[:, c, woff + 512 * js:woff + 512 * (js + 1)],
                                      start=(c == 0), stop=(c == 7)),
                             waits=w, incs=([("t", 1)] if c == 7 else []))
                    t_mm = P.n("t")
                    last_rk[gi] = t_mm
                    lastW[which] = t_mm
                    dst = r_sb if which == "r" else kpre
                    fn = AF.Sigmoid if which == "r" else AF.Relu
                    P.op("a", lambda a, bk=bk, pp=pp, js=js, dst=dst, fn=fn: a.activation(
                        dst[:, pp, 512 * js:512 * (js + 1)], pb[bk][:, :], fn),
                        waits=[("t", t_mm)] + rkt_free[pp], incs=[("s", 1)])
                    bank_free[bk] = [("s", P.n("s"))]
            s_evac = P.n("s")
            P.op("v", lambda v, pp=pp: v.tensor_tensor(
                kpre[:, pp, :], kpre[:, pp, :], kpre[:, pp, :], ALU.mult),
                waits=[("s", s_evac)] + kfm_free[pp], incs=[("v", 1)])
            sn = f"yk{pp}"
            P.op("y", YT(kfm[:, pp, :, :], kpre[:, pp, :]),
                 waits=[("v", P.n("v"))] + kfm_free[pp] +
                       ([(sn, P.n(sn))] if P.n(sn) > 0 else []),
                 incs=[(sn, 16)])
            return (sn, P.n(sn))

        def emit_body_v(gi, ti, y_k):
            tl = groups[gi][ti]
            pp = tl % 2
            for js in range(2):
                bk = 4 + js
                for c in range(8):
                    w = ([y_k, g_w[2]] + bank_free[bk]) if c == 0 else []
                    P.op("t", lambda t, bk=bk, c=c, pp=pp, js=js: t.matmul(
                        pb[bk][:, :], kfm[:, pp, c, :],
                        warena[:, c, 2 * DM + 512 * js:2 * DM + 512 * (js + 1)],
                        start=(c == 0), stop=(c == 7)),
                        waits=w, incs=([("t", 1)] if c == 7 else []))
                t_v = P.n("t")
                kfm_free[pp] = [("t", t_v)]
                lastW["v"] = t_v
                P.op("v", lambda v, bk=bk, pp=pp, js=js: v.scalar_tensor_tensor(
                    t_sb[:, pp, 512 * js:512 * (js + 1)], pb[bk][:, :], 0.5,
                    r_sb[:, pp, 512 * js:512 * (js + 1)], ALU.mult, ALU.mult),
                    waits=[("t", t_v)], incs=[("v", 1)])
                bank_free[bk] = [("v", P.n("v"))]
            P.op("v", lambda v, pp=pp, tl=tl: v.tensor_tensor(
                xr_sb[:, tl % 8, :], xr_sb[:, tl % 8, :],
                AP(t_sb, pp * DM, [[2 * DM, 128], [1, DM]]), ALU.add),
                fence=True, incs=[("v", 1)])
            xdone = P.n("v")
            rkt_free[pp] = [("v", xdone)]
            sn = f"sl{tl % 8}"
            fw = [(sn, P.n(sn))] if P.n(sn) > 0 else []
            if stage == 1:
                if STAGES == 1:
                    if OT0 <= tl <= OT1:
                        P.op("a", AD(out_d[(tl - OT0) * 128:(tl - OT0 + 1) * 128, :],
                                     xr_sb[:, tl % 8, :]),
                             waits=[("v", xdone)] + fw, incs=[(sn, 16)])
                        slot_free[tl % 8] = (sn, P.n(sn))
                    else:
                        slot_free[tl % 8] = ("v", xdone)
                else:
                    P.op("a", AD(xscr[tl * 128:(tl + 1) * 128, :], xr_sb[:, tl % 8, :]),
                         waits=[("v", xdone)] + fw, incs=[(sn, 16)])
                    slot_free[tl % 8] = (sn, P.n(sn))
                    spill_tok[tl] = (sn, P.n(sn))
            else:
                if OT0 <= tl <= OT1:
                    P.op("a", AD(out_d[(tl - OT0) * 128:(tl - OT0 + 1) * 128, :],
                                 xr_sb[:, tl % 8, :]),
                         waits=[("v", xdone)] + fw, incs=[(sn, 16)])
                    slot_free[tl % 8] = (sn, P.n(sn))
                else:
                    slot_free[tl % 8] = ("v", xdone)
            if nload[0] < len(all_tiles):
                load_x(all_tiles[nload[0]]); nload[0] += 1

        nload = [nload0]
        G = len(groups)
        for hf in range(2):
            ttok = emit_group_ln_pair(0, hf)
            emit_group_mix(0, hf, ttok)
        bnd_tok = P.n("v")
        next_ttok = {}
        pend = []
        for gi in range(G):
            for ti in range(4):
                if stage == 1 and gi == 0 and ti == 0 and STAGES > 1:
                    slot_free[0] = ("v", P.n("v"))
                    spill_tok[0] = None
                else:
                    y_k = emit_body_rk(gi, ti)
                    pend.append((gi, ti, y_k))
                if len(pend) > 1:
                    emit_body_v(*pend.pop(0))
                if gi + 1 < G and ti % 2 == 1:
                    hf = ti // 2
                    grp2 = groups[gi + 1]
                    vt = 0
                    for t2 in (2 * hf, 2 * hf + 1):
                        tl2 = grp2[t2]
                        vt = emit_ln_stats(xr_sb[:, tl2 % 8, :], t2, tl2 % 2,
                                           [xtok[tl2]])
                    stok = emit_rstd([2 * hf, 2 * hf + 1], vt)
                    tt2 = []
                    for t2 in (2 * hf, 2 * hf + 1):
                        tl3 = grp2[t2]
                        sn = f"yl{tl3 % 2}"
                        lt = emit_ln_apply(xr_sb[:, tl3 % 8, :], t2, tl3 % 2, stok)
                        P.op("y", YT(ln_fm[:, :, 16 + 128 * t2:144 + 128 * t2],
                                     ln_tm[:, tl3 % 2, :]),
                             waits=[("v", lt), ("v", bnd_tok), ("v", mix_tok[(gi, hf)])] +
                                   ([(sn, P.n(sn))] if P.n(sn) > 0 else []),
                             incs=[(sn, 16)])
                        lnT_free[tl3 % 2] = [(sn, P.n(sn))]
                        tt2.append((sn, P.n(sn)))
                    emit_group_mix(gi + 1, hf, tt2)
                    if hf == 1:
                        bnd_tok = P.n("v")
            if mid is not None and gi == 1:
                mid["tok"] = ("t", P.n("t"))
        while pend:
            emit_body_v(*pend.pop(0))
        ret = (dict(slot_free), dict(spill_tok))
        if DEBUG_DUMP and stage == 1:
            fin0 = [("t", P.n("t")), ("v", P.n("v")), ("s", P.n("s")), ("y", P.n("y"))]
            for nm, ten in (("ln_fm", ln_fm), ("xrm", xrm), ("xkm", xkm), ("r_sb", r_sb),
                            ("kpre", kpre), ("kact", kact), ("ln_tm", ln_tm),
                            ("stats", stats)):
                P.op("g", GD(dbg[nm], AP(ten, 0, [[ten.size // 128, 128],
                                                  [1, ten.size // 128]])),
                     waits=fin0, incs=[("g", 16)])
        s1.close()
        return ret

    def emit_attn_stage(barrier, t_s1_mm, t_s1, s1_slots, s1_spill):
        s2 = ExitStack()
        lsb = lambda n, s, d: s2.enter_context(nc.sbuf_tensor(n + "_s2", s, d))
        qaug = lsb("qaug", [128, 2 * DM], F16)
        kaug = lsb("kaug", [128, 2 * DM], F16)
        t1 = lsb("t1", [128, DM], F16)
        t2 = lsb("t2", [128, DM], F16)
        qpl = lsb("qpl", [128, 4, NH, 128], F16)
        kpl = lsb("kpl", [128, 4, NH, 128], F16)
        vaug = lsb("vaug", [128, 4, NH * 65], F16)
        ln2fm = lsb("ln2fm", [128, 2, 8, 128], F16)
        probsT = lsb("probsT", [128, 4, 384], BF16)
        attn_un = lsb("attn_un", [128, NH * 65], F32)
        attn_tm = lsb("attn_tm", [128, 2, DM], F16)
        attnfm = lsb("attnfm", [128, 2, 8, 128], F16)
        qm = lsb("qm", [128, 2, NH], F32)
        zs = lsb("zs", [128, 2, 2 * NH], F32)

        P.op("v", lambda v: v.memset(AP(kaug, 64, [[2 * DM, 128], [128, NH], [1, 1]]), 1.0),
             waits=list(barrier), incs=[("v", 1)])
        P.op("v", lambda v: v.memset(AP(kaug, 65, [[2 * DM, 128], [128, NH], [1, 63]]), 0.0),
             incs=[("v", 1)])
        P.op("v", lambda v: v.memset(AP(qaug, 65, [[2 * DM, 128], [128, NH], [1, 63]]), 0.0),
             incs=[("v", 1)])
        for sl in range(4):
            P.op("v", lambda v, sl=sl: v.memset(
                AP(vaug, sl * NH * 65 + 64, [[4 * NH * 65, 128], [65, NH], [1, 1]]), 1.0),
                incs=[("v", 1)])
        v_const = P.n("v")

        xtok = {}
        slot_free = dict(s1_slots or {})
        g_ffn16 = [0]
        ln2fm_free = {0: [], 1: []}
        qpl_free = {i: [] for i in range(4)}
        kpl_free = {i: [] for i in range(4)}
        vaug_free = {i: [] for i in range(4)}
        probsT_free = {i: [] for i in range(4)}
        attn_tm_free = {0: [], 1: []}
        attnfm_free = {0: [], 1: []}
        attnun_free = {0: []}
        tr_tok = {}
        v_evac = {}
        qkv_ev = {}
        wo_spill = {}
        ln2_tok = {}
        norm_tok = {}
        wo_y = {}
        exp_tok = {}
        pv_tok = {}
        sc_tok = {}
        t_qkv_last = [0]
        def load_x(tl):
            sn = f"gl{tl % 8}"
            w = [slot_free[tl % 8]] if tl % 8 in slot_free else list(barrier)
            if P.n(sn) > 0:
                w.append((sn, P.n(sn)))
            P.op("g", GD(xr_sb[:, tl % 8, :], xscr[tl * 128:(tl + 1) * 128, :]),
                 waits=w, incs=[(sn, 16)])
            xtok[tl] = (sn, P.n(sn))

        for tl in range(1, 6):
            load_x(tl)
        g_qkv = []
        for part, wkey in ((0, "r"), (1, "k"), (2, "v")):
            snq = f"awq{part}"
            P.op("a", AD(AP(warena, part * DM,
                            [[8 * 4 * DM, 128], [4 * DM, 8], [1, DM]]),
                 AP(wqkv16.tensor, part * DM,
                    [[3 * DM, 128], [128 * 3 * DM, 8], [1, DM]])),
                 waits=[("t", t_s1.get(wkey, t_s1_mm)), g_qkv16], incs=[(snq, 16)])
            g_qkv.append((snq, P.n(snq)))

        y_ln2 = {}

        def emit_ln2(j):
            par = j % 2
            sn = f"yl{par}"
            lt = emit_ln(xr_sb[:, j % 8, :], par, [xtok[j]])
            ln2_tok[j] = lt
            P.op("y", YT(ln2fm[:, par, :, :], ln_tm[:, par, :]),
                 waits=[("v", lt)] + ln2fm_free[par] +
                       ([(sn, P.n(sn))] if P.n(sn) > 0 else []),
                 incs=[(sn, 16)])
            lnT_free[par] = [(sn, P.n(sn))]
            y_ln2[j] = (sn, P.n(sn))

        def emit_qkv(j):
            par = j % 2
            y_ln = y_ln2[j]
            toks = []
            ev = []
            for js in range(6):
                bk = js % 2
                for c in range(8):
                    wm = ([y_ln, g_qkv[js // 2]] + bank_free[bk]) if c == 0 else []
                    P.op("t", lambda t, bk=bk, c=c, par=par, js=js: t.matmul(
                        pb[bk][:, :], ln2fm[:, par, c, :],
                        warena[:, c, 512 * js:512 * (js + 1)],
                        start=(c == 0), stop=False),
                        waits=wm)
                P.op("t", lambda t, bk=bk, js=js: t.matmul(
                    pb[bk][:, :], ones1[0:1, :], wqkvb_sb[0:1, 512 * js:512 * (js + 1)],
                    start=False, stop=True), incs=[("t", 1)])
                toks.append(P.n("t"))
                if js < 2:
                    dst = AP(qaug, 128 * 8 * js, [[2 * DM, 128], [128, 8], [1, 64]])
                    w = [("t", toks[js])]
                    if j - 1 in tr_tok:
                        w.append(tr_tok[j - 1])
                elif js < 4:
                    dst = AP(kaug, 128 * 8 * (js - 2), [[2 * DM, 128], [128, 8], [1, 64]])
                    w = [("t", toks[js]), ("v", v_const)]
                    if j - 1 in tr_tok:
                        w.append(tr_tok[j - 1])
                else:
                    sl = j % 4
                    dst = AP(vaug, sl * NH * 65 + 65 * 8 * (js - 4),
                             [[4 * NH * 65, 128], [65, 8], [1, 64]])
                    w = [("t", toks[js]), ("v", v_const)] + vaug_free[sl]
                P.op("a", lambda a, dst=dst, bk=bk: a.activation(dst, pb[bk][:, :], AF.Copy),
                     waits=w, incs=[("s", 1)])
                ev.append(P.n("s"))
                bank_free[bk] = [("s", P.n("s"))]
            ln2fm_free[par] = [("t", toks[-1])]
            t_qkv_last[0] = toks[-1]
            v_evac[j] = ev[5]
            qkv_ev[j] = (ev[1], ev[3])

        def emit_qkv_tail(j):
            ev1, ev3 = qkv_ev[j]
            par2 = (j + 1) % 2
            rope_tok = {}
            for which, aug, evw in (("q", qaug, ev1), ("k", kaug, ev3)):
                CC = Cq_sb if which == "q" else C_sb
                SS = Sq_sb if which == "q" else S_sb
                dv = AP(aug, 0, [[2 * DM, 128], [128, NH], [1, 64]])
                sv = AP(aug, 32, [[2 * DM, 128], [128, NH], [-32, 2], [1, 32]])
                P.op("v", lambda v, dv=dv, par2=par2, CC=CC: v.tensor_tensor(
                    t1[:, :].rearrange("p (h d) -> p h d", h=NH), dv, CC[:, par2, :]
                    .rearrange("p (h d) -> p h d", h=NH), ALU.mult),
                    waits=[("s", evw)] + ([("v", v_cq)] if which == "q" else []),
                    fence=(which == "k"), incs=[("v", 1)])
                P.op("v", lambda v, sv=sv, par2=par2, SS=SS: v.tensor_tensor(
                    t2[:, :].rearrange("p (h b d) -> p h b d", h=NH, b=2), sv,
                    SS[:, par2, :].rearrange("p (h b d) -> p h b d", h=NH, b=2), ALU.mult),
                    incs=[("v", 1)])
                P.op("v", lambda v, dv=dv: v.tensor_tensor(
                    dv, t1[:, :].rearrange("p (h d) -> p h d", h=NH),
                    t2[:, :].rearrange("p (h d) -> p h d", h=NH), ALU.add),
                    fence=True, incs=[("v", 1)])
                rope_tok[which] = P.n("v")
            P.op("a", lambda a: a.activation(qsq[:, :],
                                             AP(qaug, 0, [[2 * DM, 128], [128, NH], [1, 64]]),
                                             AF.Square),
                 waits=[("v", rope_tok["q"])] + qsq_free[0] + qsq_waw[:], incs=[("s", 1)])
            qsq_waw[:] = [("s", P.n("s"))]
            s_sq = P.n("s")
            P.op("v", lambda v: v.tensor_reduce(qm[:, 0, :],
                                                qsq[:, :].rearrange("p (h d) -> p h d", h=NH),
                                                AX.X, ALU.add),
                 waits=[("s", s_sq)], incs=[("v", 1)])
            qsq_free[0] = [("v", P.n("v"))]
            P.op("v", lambda v: v.tensor_scalar(qm[:, 0, :], qm[:, 0, :], 1e-12, None, ALU.max),
                 fence=True, incs=[("v", 1)])
            P.op("a", lambda a: a.activation(qm[:, 1, :], qm[:, 0, :], AF.Ln),
                 waits=[("v", P.n("v"))], incs=[("s", 1)])
            P.op("a", lambda a: a.activation(qm[:, 1, :], qm[:, 1, :], AF.Exp, scale=0.5),
                 waits=[("s", P.n("s"))], incs=[("s", 1)])
            s_nrm = P.n("s")
            with nc.allow_low_precision(reason="stabilizer M col; any nearby value is valid"):
                P.op("v", lambda v: v.tensor_tensor(
                    AP(qaug, 64, [[2 * DM, 128], [128, NH], [1, 1]]),
                    qm[:, 1, :], nkb2b[:, :], ALU.mult),
                    waits=[("s", s_nrm), ("v", v_pre)], incs=[("v", 1)])
            m_tok = P.n("v")
            sn = f"yt{j % 4}"
            P.op("y", YT(qpl[:, j % 4, :, :], qaug[:, :]),
                 waits=[("v", m_tok)] + qpl_free[j % 4] +
                       ([(sn, P.n(sn))] if P.n(sn) > 0 else []),
                 incs=[(sn, 16)])
            P.op("y", YT(kpl[:, j % 4, :, :], kaug[:, :]),
                 waits=[("v", rope_tok["k"])] + kpl_free[j % 4], incs=[(sn, 16)])
            tr_tok[j] = (sn, P.n(sn))

        def emit_attn_core(i):
            par = i % 2
            win = [(i - 1) % 4, i % 4, (i + 1) % 4]
            gi = 1 if i == 3 else (2 if i == 18 else 0)
            LAG = 2
            for hh in range(NH + LAG):
                if hh < NH:
                    h = hh
                    hp = h % 4
                    bk_sc = 2 if h % 2 == 0 else 7
                    w0 = [tr_tok[i - 1], tr_tok[i], tr_tok[i + 1], ("s", v_evac[i + 1])] \
                        if h == 0 else []
                    if (i, h - 2) in exp_tok:
                        w0 = w0 + [("s", exp_tok[(i, h - 2)])]
                    elif h < 2 and (i - 1, NH - 2 + h) in exp_tok:
                        w0 = w0 + [("s", exp_tok[(i - 1, NH - 2 + h)])]
                    for ci, sl in enumerate(win):
                        w = (w0 + bank_free[bk_sc]) if ci == 0 else []
                        P.op("t", lambda t, sl=sl, h=h, ci=ci, par=par, bk=bk_sc: t.matmul(
                            pb[bk][:, ci * 128:(ci + 1) * 128],
                            AP(kpl, sl * NH * 128 + h * 128, [[4 * NH * 128, 65], [1, 128]]),
                            AP(qpl, (i % 4) * NH * 128 + h * 128, [[4 * NH * 128, 65], [1, 128]]),
                            start=True, stop=True),
                            waits=w, incs=([("t", 1)] if ci == 2 else []))
                    sc_tok[(i, h)] = P.n("t")
                    bank_free[bk_sc] = []
                    wp = probsT_free[hp]
                    P.op("a", lambda a, h=h, bk=bk_sc, hp=hp: a.activation(
                        probsT[:, hp, :], pb[bk][:, 0:384], AF.Exp),
                        waits=[("t", sc_tok[(i, h)])] + wp, incs=[("s", 1)])
                    exp_tok[(i, h)] = P.n("s")
                    bank_free[bk_sc] = [("s", exp_tok[(i, h)])]
                    P.op("v", lambda v, hp=hp, gi=gi: v.tensor_tensor(
                        AP(probsT, hp * 384, [[4 * 384, 128], [256, 2], [1, 128]]),
                        AP(probsT, hp * 384, [[4 * 384, 128], [256, 2], [1, 128]]),
                        gates_sb[:, gi, :].rearrange("p (b d) -> p b d", b=2), ALU.mult),
                        waits=[("s", exp_tok[(i, h)]), a_misc], incs=[("v", 1)])
                    exp_tok[(i, h, "g")] = P.n("v")
                h2 = hh - LAG
                if h2 >= 0:
                    h = h2
                    hp = h % 4
                    bk_pv = 3 if h % 2 == 0 else 6
                    w = [("v", exp_tok[(i, h, "g")])] + bank_free[bk_pv]
                    for ci, sl in enumerate(win):
                        P.op("t", lambda t, sl=sl, h=h, ci=ci, hp=hp, bk=bk_pv: t.matmul(
                            pb[bk][:, 0:65],
                            AP(probsT, hp * 384 + ci * 128, [[4 * 384, 128], [1, 128]]),
                            AP(vaug, sl * NH * 65 + h * 65, [[4 * NH * 65, 128], [1, 65]]),
                            start=(ci == 0), stop=(ci == 2)),
                            waits=(w if ci == 0 else []), incs=([("t", 1)] if ci == 2 else []))
                    pv_tok[(i, h)] = P.n("t")
                    probsT_free[hp] = [("t", pv_tok[(i, h)])]
                    wv = attnun_free[0] if h <= 1 else []
                    if h % 2 == 0:
                        P.op("a", lambda a, h=h, bk=bk_pv: a.activation(
                            attn_un[:, h * 65:(h + 1) * 65], pb[bk][:, 0:65], AF.Copy),
                            waits=[("t", pv_tok[(i, h)])] + wv, incs=[("s", 1)])
                        bank_free[bk_pv] = [("s", P.n("s"))]
                    else:
                        P.op("v", lambda v, h=h, bk=bk_pv: v.tensor_copy(
                            attn_un[:, h * 65:(h + 1) * 65], pb[bk][:, 0:65]),
                            waits=[("t", pv_tok[(i, h)])] + wv, incs=[("v", 1)])
                        bank_free[bk_pv] = [("v", P.n("v"))]
            t_last = pv_tok[(i, NH - 1)]
            for sl in win:
                kpl_free[sl] = [("t", sc_tok[(i, NH - 1)])]
                vaug_free[sl] = [("t", t_last)]
            qpl_free[i % 4] = [("t", sc_tok[(i, NH - 1)])]
            P.op("v", lambda v, par=par: v.tensor_scalar(
                zs[:, par, 0:NH], AP(attn_un, 64, [[NH * 65, 128], [65, NH], [1, 1]]),
                1e-30, None, ALU.max),
                waits=[("s", P.n("s")), ("v", P.n("v"))], incs=[("v", 1)])
            P.op("v", lambda v, par=par: v.reciprocal(zs[:, par, NH:2 * NH], zs[:, par, 0:NH]),
                 fence=True, incs=[("v", 1)])
            for h in range(NH):
                P.op("v", lambda v, h=h, par=par: v.tensor_scalar(
                    attn_tm[:, par, h * 64:(h + 1) * 64],
                    attn_un[:, h * 65:h * 65 + 64],
                    zs[:, par, NH + h:NH + h + 1], None, ALU.mult),
                    waits=(attn_tm_free[par] if h == 0 else []), fence=(h == 0),
                    incs=([("v", 1)] if h == NH - 1 else []))
            norm_tok[i] = P.n("v")
            attnun_free[0] = [("v", norm_tok[i])]
            sn = f"yaf{par}"
            P.op("y", YT(attnfm[:, par, :, :], attn_tm[:, par, :]),
                 waits=[("v", norm_tok[i])] + attnfm_free[par] +
                       ([(sn, P.n(sn))] if P.n(sn) > 0 else []),
                 incs=[(sn, 16)])
            attn_tm_free[par] = [(sn, P.n(sn))]
            wo_y[i] = (sn, P.n(sn))

        def emit_wo(i):
            par = i % 2
            for js in range(2):
                bk = 4 + js
                for c in range(8):
                    w = ([wo_y[i], g_wo] + bank_free[bk]) if c == 0 else []
                    P.op("t", lambda t, bk=bk, c=c, par=par, js=js: t.matmul(
                        pb[bk][:, :], attnfm[:, par, c, :],
                        warena[:, c, 3 * DM + 512 * js:3 * DM + 512 * (js + 1)],
                        start=(c == 0), stop=False),
                        waits=w)
                P.op("t", lambda t, bk=bk, js=js: t.matmul(
                    pb[bk][:, :], ones1[0:1, :], wob_sb[0:1, 512 * js:512 * (js + 1)],
                    start=False, stop=True), incs=[("t", 1)])
                t_wo = P.n("t")
                if js == 1:
                    attnfm_free[par] = [("t", t_wo)]
                P.op("v", lambda v, bk=bk, i=i, js=js: v.tensor_tensor(
                    AP(xr_sb, (i % 8) * DM + 512 * js, [[8 * DM, 128], [1, 512]]),
                    pb[bk][:, :],
                    AP(xr_sb, (i % 8) * DM + 512 * js, [[8 * DM, 128], [1, 512]]),
                    ALU.add),
                    waits=[("t", t_wo), ("v", ln2_tok[i])], incs=[("v", 1)])
                bank_free[bk] = [("v", P.n("v"))]
            x2_tok = P.n("v")
            sn = f"gl{i % 8}"
            fw = [(sn, P.n(sn))] if P.n(sn) > 0 else []
            if STAGES == 3 and OT0 <= i <= OT1:
                P.op("g", GD(out_d[(i - OT0) * 128:(i - OT0 + 1) * 128, :], xr_sb[:, i % 8, :]),
                     waits=[("v", x2_tok)] + fw, incs=[(sn, 16)])
            else:
                P.op("g", GD(xscr[i * 128:(i + 1) * 128, :], xr_sb[:, i % 8, :]),
                     waits=[("v", x2_tok)] + fw, incs=[(sn, 16)])
            slot_free[i % 8] = (sn, P.n(sn))
            wo_spill[i] = (sn, P.n(sn))

        emit_ln2(1)
        for j in range(1, NT):
            emit_qkv(j)
            if QT0 <= j - 2 <= QT1:
                emit_attn_core(j - 2)
            emit_qkv_tail(j)
            if j + 1 < NT:
                emit_ln2(j + 1)
            if j == 1 or j == 19:
                slot_free[j % 8] = ("v", ln2_tok[j])
            if QT0 <= j - 3 <= QT1:
                emit_wo(j - 3)
            if j == 6:
                for ii, wd in enumerate((wr_d, wk_d, wv_d)):
                    P.op("g", GD(ffn16[:, ii * DM:(ii + 1) * DM], wd),
                         waits=([("t", P.n("t"))] if ii == 0 else []), incs=[("gf6", 16)])
                g_ffn16[0] = P.n("gf6")
            if j + 5 < NT:
                load_x(j + 5)
        g_w4 = []
        for sn4, i in (("aw4r", 0), ("aw4k", 1), ("aw4v", 2)):
            P.op("a", AD(warena[:, :, i * DM:(i + 1) * DM],
                         ffn16[:, i * DM:(i + 1) * DM].rearrange("(c p) d -> p c d", p=128)),
                 waits=([("t", t_qkv_last[0]), ("gf6", g_ffn16[0])] if i == 0 else []),
                 incs=[(sn4, 16)])
            g_w4.append((sn4, P.n(sn4)))
        emit_attn_core(18)
        emit_wo(17)
        emit_wo(18)
        s2.close()
        return g_w4, dict(slot_free), dict(wo_spill)

    s1_mid = {}
    s1_lastW = {}
    s1_slots, s1_spill = emit_ffn_stage(1, barrier=[], g_w=g_ffnw, mid=s1_mid, lastW=s1_lastW)
    t_s1_mm = P.n("t")
    P.op("g", GD(wqkv16[:, :], wqkv_d), waits=[s1_mid.get("tok", ("t", 0))],
         incs=[("gq6", 16)])
    g_qkv16 = ("gq6", P.n("gq6"))
    mkbar = lambda: ([("t", P.n("t")), ("v", P.n("v")), ("s", P.n("s"))] +
                     [(nm, P.n(nm)) for nm in STREAMS if P.n(nm) > 0])
    if STAGES > 1:
        bar1 = mkbar()
        g_w4, s23_slots, s23_spill = emit_attn_stage(bar1, t_s1_mm, s1_lastW, s1_slots, s1_spill)
        if STAGES > 3:
            bar2 = mkbar()
            emit_ffn_stage(4, barrier=bar2, g_w=g_w4, slot_init=s23_slots, spill_prev=s23_spill)

    fin = [(nm, P.n(nm)) for nm in
           (["a", "y", "t", "v", "s", "g",
             "gm", "gw0", "gw1", "gw2", "gwo", "gq6", "gf6"] + STREAMS)
           if P.n(nm) > 0]
    P.op("g", lambda g: None, waits=fin)

    with nc.Block() as block, ExitStack() as semstack:
        sems = {}
        for s in P.cnt:
            sems[s] = semstack.enter_context(nc.semaphore(s))

        @block.tensor
        def _(t):
            P.replay("t", t, sems)

        @block.vector
        def _(v):
            P.replay("v", v, sems)

        @block.scalar
        def _(a):
            P.replay("a", a, sems)

        @block.gpsimd
        def _(g):
            P.replay("g", g, sems)

        @block.sync
        def _(y):
            P.replay("y", y, sems)

    st.close()
    return nc


def host_prep(inputs):
    x = np.ascontiguousarray(inputs["L0_attn_hidden_states"], dtype=np.float32)
    wqkv = np.ascontiguousarray(inputs["L0_attn_Wqkv_kernel"], dtype=np.float32)
    wqkvb = np.ascontiguousarray(inputs["L0_attn_Wqkv_bias"], dtype=np.float32)
    wo = np.ascontiguousarray(inputs["L0_attn_Wo_kernel"], dtype=np.float32)
    wob = np.ascontiguousarray(inputs["L0_attn_Wo_bias"], dtype=np.float32)
    rope = np.ascontiguousarray(inputs["L0_attn_rope_cache"], dtype=np.float32)
    wr = np.ascontiguousarray(inputs["L0_ffn_W_r"], dtype=np.float32).T.copy()
    wk = np.ascontiguousarray(inputs["L0_ffn_W_k"], dtype=np.float32).T.copy()
    wv = np.ascontiguousarray(inputs["L0_ffn_W_v"], dtype=np.float32).T.copy()
    mu_r = np.ascontiguousarray(inputs["L0_ffn_mu_r"], dtype=np.float32)
    mu_k = np.ascontiguousarray(inputs["L0_ffn_mu_k"], dtype=np.float32)

    perm64 = np.concatenate([np.arange(0, 64, 2), np.arange(1, 64, 2)])
    permh = np.concatenate([h * 64 + perm64 for h in range(NH)])
    colperm = np.concatenate([permh, 1024 + permh, np.arange(2048, 3072)])
    wqkv_p = wqkv[:, colperm].copy()
    wqkvb_p = wqkvb[colperm][None, :].copy()

    mu = np.concatenate([mu_r.reshape(8, 128).T, mu_k.reshape(8, 128).T], axis=1).copy()

    cos = rope[:, :, 0]
    sin = rope[:, :, 1]
    cs_head = np.concatenate([cos, cos], axis=1)
    ss_head = np.concatenate([sin, sin], axis=1)
    C = np.tile(cs_head, (1, NH)).reshape(2, 128, 2 * 512)
    S = np.tile(ss_head, (1, NH)).reshape(2, 128, 2 * 512)
    cosx = np.concatenate([C[0], C[1]], axis=1).copy()
    sinx = np.concatenate([S[0], S[1]], axis=1).copy()
    pm1 = np.tile(np.concatenate([-np.ones(32, np.float32), np.ones(32, np.float32)]), NH)
    pm1 = np.tile(pm1[None, :], (128, 1)).astype(ml_dtypes.float16 if False else np.float16)

    i = np.arange(128)
    gA = (i[:, None] >= i[None, :]).astype(np.float32)
    gC = (i[:, None] <= i[None, :]).astype(np.float32)
    interior = np.concatenate([gA, gC], axis=1)
    first = np.concatenate([np.zeros_like(gA), gC], axis=1)
    last = np.concatenate([gA, np.zeros_like(gC)], axis=1)

    in_maps = []
    for core in range(8):
        b, h = core // 2, core % 2
        S0 = h * 2048
        lo, hi = S0 - 384, S0 + 2176
        lo_c, hi_c = max(lo, 0), min(hi, 4096)
        xs = np.zeros((TW, DM), np.float32)
        xs[lo_c - lo:hi_c - lo] = x[b, lo_c:hi_c]
        g1 = first if h == 0 else interior
        g2 = interior if h == 0 else last
        gates = np.stack([interior, g1, g2], axis=1).astype(ml_dtypes.bfloat16)
        bsel = np.full((128, 1), float(h), np.float32)
        in_maps.append(dict(
            x0=xs, wqkv=wqkv_p, wqkvb=wqkvb_p, wo=wo, wob=wob[None, :].copy(),
            wr=wr, wk=wk, wv=wv, mu=mu, cosx=cosx, sinx=sinx, pm1=pm1,
            gates=np.ascontiguousarray(gates), bsel=bsel))
    return in_maps


_NC_CACHE = None


def kernel(**inputs):
    global _NC_CACHE
    in_maps = host_prep(inputs)
    if _NC_CACHE is None:
        _NC_CACHE = build_nc()
    res = run_bass_kernel_spmd(_NC_CACHE, in_maps, core_ids=list(range(8)))
    out = np.empty((4, 4096, DM), np.float32)
    for core in range(8):
        b, h = core // 2, core % 2
        out[b, h * 2048:(h + 1) * 2048] = res.results[core]["out"]
    return out


# version 28
# speedup vs baseline: 1.0844x; 1.0844x over previous
import numpy as np
import ml_dtypes
import concourse.bass as bass
import concourse.mybir as mybir
from concourse.bass_utils import run_bass_kernel_spmd
from contextlib import ExitStack

F32 = mybir.dt.float32
F16 = mybir.dt.float16
BF16 = mybir.dt.bfloat16
AF = mybir.ActivationFunctionType
ALU = mybir.AluOpType
AX = mybir.AxisListType
AP = bass.AP

NH, HD, DM = 16, 64, 1024
NT = 20
TW = NT * 128
QT0, QT1 = 2, 18
OT0, OT1 = 3, 18
DEBUG_DUMP = False
STAGES = 4


class Prog:
    def __init__(self, nc):
        self.nc = nc
        self.qs = {k: [] for k in "tvagy"}
        self.cnt = {}

    def add(self, name):
        self.cnt[name] = 0

    def op(self, q, build, waits=(), incs=(), fence=False):
        waits = [(s, v) for (s, v) in waits if v > 0]
        if fence and self.qs[q]:
            prev = self.qs[q][-1]
            if prev[2]:
                sem, amt, cum = prev[2][0]
                waits.append((sem, cum))
            else:
                fs = "f" + q
                self.cnt[fs] += 1
                prev[2].append((fs, 1, self.cnt[fs]))
                waits.append((fs, self.cnt[fs]))
        rec = []
        for s, a in incs:
            self.cnt[s] += a
            rec.append((s, a, self.cnt[s]))
        self.qs[q].append([waits, build, rec])

    def n(self, sem):
        return self.cnt[sem]

    def replay(self, q, eng, sems):
        for waits, build, incs in self.qs[q]:
            for s, v in waits:
                eng.wait_ge(sems[s], v)
            inst = build(eng)
            for s, a, _ in incs:
                if inst is None:
                    eng.sem_inc(sems[s], a)
                else:
                    inst.then_inc(sems[s], a)


def build_nc():
    nc = bass.Bass("TRN2", target_bir_lowering=False, debug=False)
    dt = nc.dram_tensor
    x0_d = dt("x0", [TW, DM], F32, kind="ExternalInput").ap()
    wqkv_d = dt("wqkv", [DM, 3 * DM], F32, kind="ExternalInput").ap()
    wqkvb_d = dt("wqkvb", [1, 3 * DM], F32, kind="ExternalInput").ap()
    wo_d = dt("wo", [DM, DM], F32, kind="ExternalInput").ap()
    wob_d = dt("wob", [1, DM], F32, kind="ExternalInput").ap()
    wr_d = dt("wr", [DM, DM], F32, kind="ExternalInput").ap()
    wk_d = dt("wk", [DM, DM], F32, kind="ExternalInput").ap()
    wv_d = dt("wv", [DM, DM], F32, kind="ExternalInput").ap()
    mu_d = dt("mu", [128, 16], F32, kind="ExternalInput").ap()
    cos_d = dt("cosx", [128, 2 * DM], F32, kind="ExternalInput").ap()
    sin_d = dt("sinx", [128, 2 * DM], F32, kind="ExternalInput").ap()
    pm1_d = dt("pm1", [128, DM], F16, kind="ExternalInput").ap()
    gates_d = dt("gates", [128, 3, 256], BF16, kind="ExternalInput").ap()
    bsel_d = dt("bsel", [128, 1], F32, kind="ExternalInput").ap()
    nout = (OT1 - OT0 + 1) * 128
    out_d = dt("out", [nout, DM], F32, kind="ExternalOutput").ap()
    xscr = dt("xscr", [TW, DM], F32).ap()
    wqkv16 = dt("wqkv16", [DM, 3 * DM], F16).ap()
    ffn16 = dt("ffn16", [DM, 3 * DM], F16).ap()
    dbg = {}
    if DEBUG_DUMP:
        for nm, shape in (("ln_fm", [128, 8 * 528]), ("xrm", [128, 8 * 512]),
                          ("xkm", [128, 8 * 512]), ("r_sb", [128, 2 * DM]),
                          ("kpre", [128, 2 * DM]), ("kact", [128, 2 * DM]),
                          ("ln_tm", [128, 2 * DM]), ("stats", [128, 16])):
            dbg[nm] = dt("dbg_" + nm, shape, F32, kind="ExternalOutput").ap()

    st = ExitStack()
    sb = lambda n, s, d: st.enter_context(nc.sbuf_tensor(n, s, d))

    xr_sb = sb("xr_sb", [128, 8, DM], F32)
    warena = sb("warena", [128, 8, 4 * DM], F16)
    C_sb = sb("C_sb", [128, 2, DM], F16)
    S_sb = sb("S_sb", [128, 2, DM], F16)
    Cq_sb = sb("Cq_sb", [128, 2, DM], F16)
    Sq_sb = sb("Sq_sb", [128, 2, DM], F16)
    pm1_sb = sb("pm1_sb", [128, DM], F16)
    gates_sb = sb("gates_sb", [128, 3, 256], BF16)
    mu_sb = sb("mu_sb", [128, 16], F32)
    negmu = sb("negmu", [128, 16], F32)
    wqkvb_sb = sb("wqkvb_sb", [1, 3 * DM], F16)
    wob_sb = sb("wob_sb", [1, DM], F16)
    ones1 = sb("ones1", [1, 128], F16)
    bsel_sb = sb("bsel_sb", [128, 1], F32)
    kb2s = sb("kb2s", [1, 2 * NH], F32)
    kb2row = sb("kb2row", [1, NH], F16)
    nkb2b = sb("nkb2b", [128, NH], F32)
    bsq = sb("bsq", [1, DM], F32)
    ln_tm = sb("ln_tm", [128, 2, DM], F16)
    stats = sb("stats", [128, 4, 8], F32)
    qsq = sb("qsq", [128, DM], F16)
    qsq_free = {0: []}
    qsq_waw = []
    stats_free = {}
    pb = [st.enter_context(nc.psum_tensor(f"pb{i}", [128, 512], F32)) for i in range(8)]

    P = Prog(nc)
    STREAMS = (["am", "awq0", "awq1", "awq2", "aw4r", "aw4k", "aw4v", "ylb"] +
               [f"sl{i}" for i in range(8)] + [f"gl{i}" for i in range(8)] +
               ["yl0", "yl1", "yk0", "yk1", "yt0", "yt1", "yt2", "yt3", "yaf0", "yaf1",
                "gw0a", "gw0b"])
    for s in (["g", "a", "y", "t", "v", "s", "ft", "fv", "fa", "fg", "fy",
               "gm", "gw0", "gw1", "gw2", "gwo", "gq6", "gf6"] + STREAMS):
        P.add(s)

    GD = lambda o, i: (lambda g: g.dma_start(o, i))
    AD = lambda o, i: (lambda a: a.dma_start(o, i))
    YT = lambda o, i: (lambda y: y.dma_start(o, i, transpose=True))

    bank_free = {i: [] for i in range(8)}
    lnT_free = {0: [], 1: []}

    g_ffnw3 = []
    for hf, snw in ((0, "gw0a"), (1, "gw0b")):
        P.op("g", GD(warena[:, :, 512 * hf:512 * (hf + 1)],
                     AP(wr_d.tensor, 512 * hf, [[DM, 128], [128 * DM, 8], [1, 512]])),
             incs=[(snw, 16)])
    g_ffnw3.append([("gw0a", 16), ("gw0b", 16)])
    for o, i in ((wqkvb_sb[:, :], wqkvb_d), (wob_sb[:, :], wob_d), (C_sb[:, :, :], cos_d),
                 (S_sb[:, :, :], sin_d), (pm1_sb[:, :], pm1_d)):
        P.op("g", GD(o, i), incs=[("gm", 16)])
    g_misc = ("gm", P.n("gm"))
    for nmw, (i, wd) in (("gw1", (1, wk_d)), ("gw2", (2, wv_d))):
        P.op("g", GD(warena[:, :, i * DM:(i + 1) * DM],
                     wd.rearrange("(c p) d -> p c d", p=128)), incs=[(nmw, 16)])
        g_ffnw3.append((nmw, P.n(nmw)))
    g_ffnw = list(g_ffnw3)
    P.op("g", GD(warena[:, :, 3 * DM:4 * DM], wo_d.rearrange("(c p) d -> p c d", p=128)),
         incs=[("gwo", 16)])
    g_wo = ("gwo", P.n("gwo"))
    for o, i in ((gates_sb[:, :, :], gates_d), (mu_sb[:, :], mu_d), (bsel_sb[:, :], bsel_d)):
        P.op("a", AD(o, i), incs=[("am", 16)])
    a_misc = ("am", P.n("am"))

    P.op("v", lambda v: v.memset(ones1[:, :], 1.0), incs=[("v", 1)])
    P.op("v", lambda v: v.tensor_scalar(negmu[:, :], mu_sb[:, :], 1.0, None, ALU.subtract),
         waits=[a_misc], incs=[("v", 1)])
    for par in range(2):
        P.op("v", lambda v, par=par: v.tensor_tensor(S_sb[:, par, :], S_sb[:, par, :],
                                                     pm1_sb[:, :], ALU.mult),
             waits=[g_misc], incs=[("v", 1)])
    P.op("v", lambda v: v.tensor_scalar(Cq_sb[:, :, :], C_sb[:, :, :], 0.125, None, ALU.mult),
         waits=[g_misc], incs=[("v", 1)])
    P.op("v", lambda v: v.tensor_scalar(Sq_sb[:, :, :], S_sb[:, :, :], 0.125, None, ALU.mult),
         fence=True, incs=[("v", 1)])
    v_cq = P.n("v")
    P.op("a", lambda a: a.activation(bsq[:, :], wqkvb_sb[:, DM:2 * DM], AF.Square),
         waits=[g_misc, ("v", P.n("v"))], incs=[("s", 1)])
    P.op("v", lambda v: v.tensor_reduce(kb2s[:, 0:NH],
                                        bsq[:, :].rearrange("p (h d) -> p h d", h=NH),
                                        AX.X, ALU.add),
         waits=[("s", P.n("s"))], incs=[("v", 1)])
    P.op("v", lambda v: v.tensor_scalar(kb2s[:, 0:NH], kb2s[:, 0:NH], 1e-12, None, ALU.max),
         fence=True, incs=[("v", 1)])
    P.op("a", lambda a: a.activation(kb2s[:, NH:2 * NH], kb2s[:, 0:NH], AF.Ln),
         waits=[("v", P.n("v"))], incs=[("s", 1)])
    P.op("a", lambda a: a.activation(kb2s[:, NH:2 * NH], kb2s[:, NH:2 * NH], AF.Exp, scale=0.5),
         waits=[("s", P.n("s"))], incs=[("s", 1)])
    with nc.allow_low_precision(reason="stabilizer scale; any nearby value is valid"):
        P.op("v", lambda v: v.tensor_scalar(kb2row[:, :], kb2s[:, NH:2 * NH], 14.2, -1.0,
                                            ALU.add, ALU.mult),
             waits=[("s", P.n("s"))], incs=[("v", 1)])
    P.op("t", lambda t: t.matmul(pb[7][:, 0:NH], ones1[0:1, :], kb2row[0:1, :],
                                 start=True, stop=True),
         waits=[("v", P.n("v"))], incs=[("t", 1)])
    P.op("v", lambda v: v.tensor_copy(nkb2b[:, :], pb[7][:, 0:NH]),
         waits=[("t", P.n("t"))], incs=[("v", 1)])
    P.op("v", lambda v: v.tensor_scalar(nkb2b[:, :], nkb2b[:, :], 0.125, None, ALU.mult),
         fence=True, incs=[("v", 1)])
    bank_free[7] = [("v", P.n("v"))]
    v_pre = P.n("v")

    def emit_ln_stats(x_ap, slot, par, waits):
        sl = lambda c: stats[:, slot, c:c + 1]
        P.op("a", lambda a: a.activation(ln_tm[:, par, :], x_ap, AF.Copy, accum_out=sl(0)),
             waits=list(waits) + lnT_free[par] + stats_free.pop(slot, []),
             incs=[("s", 1)])
        P.op("a", lambda a: a.activation(qsq[:, :], x_ap, AF.Square, accum_out=sl(1)),
             waits=list(qsq_free[0]) + qsq_waw[:], incs=[("s", 1)])
        qsq_waw[:] = [("s", P.n("s"))]
        s_sq = P.n("s")
        P.op("v", lambda v: v.tensor_scalar(sl(2), sl(0), 1.0 / DM, None, ALU.mult),
             waits=[("s", s_sq)], incs=[("v", 1)])
        P.op("v", lambda v: v.tensor_tensor(sl(3), sl(2), sl(2), ALU.mult), fence=True,
             incs=[("v", 1)])
        P.op("v", lambda v: v.scalar_tensor_tensor(sl(4), sl(1), 1.0 / DM, sl(3),
                                                   ALU.mult, ALU.subtract),
             fence=True, incs=[("v", 1)])
        P.op("v", lambda v: v.tensor_scalar(sl(4), sl(4), 1e-5, None, ALU.add),
             fence=True, incs=[("v", 1)])
        return P.n("v")

    def emit_rstd(slots, wtok):
        lo, n = min(slots), len(slots)
        lnv = AP(stats, lo * 8 + 6, [[4 * 8, 128], [8, n], [1, 1]])
        var = AP(stats, lo * 8 + 4, [[4 * 8, 128], [8, n], [1, 1]])
        rst = AP(stats, lo * 8 + 5, [[4 * 8, 128], [8, n], [1, 1]])
        P.op("a", lambda a: a.activation(lnv, var, AF.Ln), waits=[("v", wtok)], incs=[("s", 1)])
        P.op("a", lambda a: a.activation(rst, lnv, AF.Exp, scale=-0.5),
             waits=[("s", P.n("s"))], incs=[("s", 1)])
        return P.n("s")

    def emit_ln_apply(x_ap, slot, par, stok, waits=()):
        sl = lambda c: stats[:, slot, c:c + 1]
        P.op("v", lambda v: v.tensor_scalar(ln_tm[:, par, :], x_ap, sl(2), sl(5),
                                            ALU.subtract, ALU.mult),
             waits=[("s", stok)] + list(waits) + lnT_free[par], incs=[("v", 1)])
        stats_free[slot] = [("v", P.n("v"))]
        return P.n("v")

    def emit_ln(x_ap, par, waits):
        wtok = emit_ln_stats(x_ap, par, par, waits)
        stok = emit_rstd([par], wtok)
        return emit_ln_apply(x_ap, par, par, stok)

    def emit_ffn_stage(stage, barrier, g_w, mid=None, lastW=None, slot_init=None, spill_prev=None):
        if lastW is None:
            lastW = {}
        s1 = ExitStack()
        lsb = lambda n, s, d: s1.enter_context(nc.sbuf_tensor(n + f"_s{stage}", s, d))
        ln_fm = lsb("ln_fm", [128, 8, 528], F16)
        d_fm = lsb("d_fm", [128, 8, 512], F16)
        xrm2 = lsb("xrm2", [128, 2, 8, 512], F16)
        xkm2 = lsb("xkm2", [128, 2, 8, 512], F16)
        r_sb = lsb("r_sb", [128, 2, DM], F16)
        kpre = lsb("kpre", [128, 2, DM], F16)
        kfm = lsb("kfm", [128, 2, 8, 128], F16)
        t_sb = lsb("t_sb", [128, 2, DM], F32)

        if stage == 1:
            src = x0_d
            groups = [list(range(4 * g, 4 * g + 4)) for g in range(5)]
            bnd = None
        else:
            src = xscr
            groups = [[3, 4, 5, 6], [7, 8, 9, 10], [11, 12, 13, 14], [15, 16, 17, 18]]
            bnd = 2
        all_tiles = ([bnd] if bnd is not None else []) + [t for g in groups for t in g]

        xtok = {}
        slot_free = dict(slot_init or {})
        spill_tok = {}
        kfm_free = {0: [], 1: []}
        rkt_free = {0: [], 1: []}

        qd, pfx = ("a", "sl") if stage == 1 else ("g", "gl")

        def load_x(tl):
            sn = f"{pfx}{tl % 8}"
            w = ([slot_free[tl % 8]] if tl % 8 in slot_free else list(barrier))
            if P.n(sn) > 0:
                w.append((sn, P.n(sn)))
            if spill_prev is not None and spill_prev.get(tl):
                w.append(spill_prev[tl])
            P.op(qd, (AD if qd == "a" else GD)(xr_sb[:, tl % 8, :],
                                               src[tl * 128:(tl + 1) * 128, :]),
                 waits=w, incs=[(sn, 16)])
            xtok[tl] = (sn, P.n(sn))

        nload0 = 0
        for tl in all_tiles[:8]:
            load_x(tl); nload0 += 1

        bcol = AP(ln_fm, 15, [[8 * 528, 128], [528, 8], [1, 1]])
        if bnd is not None:
            lnb = lsb("lnb", [128, 8, 128], F16)
            lt = emit_ln(xr_sb[:, bnd % 8, :], bnd % 2, [xtok[bnd]])
            P.op("y", YT(lnb[:, :, :], ln_tm[:, bnd % 2, :]), waits=[("v", lt)],
                 incs=[("ylb", 16)])
            lnT_free[bnd % 2] = [("ylb", P.n("ylb"))]
            P.op("v", lambda v: v.tensor_scalar(
                bcol, AP(lnb, 127, [[8 * 128, 128], [128, 8], [1, 1]]),
                bsel_sb[:, 0:1], None, ALU.mult),
                waits=[("ylb", P.n("ylb")), a_misc], incs=[("v", 1)])
            slot_free[bnd % 8] = ("v", P.n("v"))
        else:
            P.op("v", lambda v: v.memset(bcol, 0.0), waits=list(barrier), incs=[("v", 1)])
        bnd_tok = P.n("v")
        mix_tok = {}
        last_rk = {}

        def emit_group_ln_pair(gi, hf):
            grp = groups[gi]
            vt = 0
            for ti in (2 * hf, 2 * hf + 1):
                tl = grp[ti]
                vt = emit_ln_stats(xr_sb[:, tl % 8, :], ti, tl % 2, [xtok[tl]])
            stok = emit_rstd([2 * hf, 2 * hf + 1], vt)
            ttok = []
            for ti in (2 * hf, 2 * hf + 1):
                tl = grp[ti]
                sn = f"yl{tl % 2}"
                lt = emit_ln_apply(xr_sb[:, tl % 8, :], ti, tl % 2, stok)
                P.op("y", YT(ln_fm[:, :, 16 + 128 * ti:144 + 128 * ti], ln_tm[:, tl % 2, :]),
                     waits=[("v", lt), ("v", bnd_tok)] +
                           ([(sn, P.n(sn))] if P.n(sn) > 0 else []) +
                           ([("v", mix_tok[(gi - 1, hf)])] if (gi - 1, hf) in mix_tok else []),
                     incs=[(sn, 16)])
                lnT_free[tl % 2] = [(sn, P.n(sn))]
                ttok.append((sn, P.n(sn)))
            return ttok

        def emit_group_mix(gi, hf, ttok):
            gp = gi % 2
            o = 256 * hf
            w = list(ttok) + [a_misc] + ([("t", last_rk[gi - 2])] if gi - 2 in last_rk else [])
            P.op("v", lambda v, o=o: v.tensor_tensor(
                d_fm[:, :, o:o + 256], ln_fm[:, :, 16 + o:272 + o],
                ln_fm[:, :, 15 + o:271 + o], ALU.subtract),
                waits=w, incs=[("v", 1)])
            for c in range(8):
                P.op("v", lambda v, c=c, gp=gp, o=o: v.scalar_tensor_tensor(
                    xrm2[:, gp, c, o:o + 256], d_fm[:, c, o:o + 256], negmu[:, c:c + 1],
                    ln_fm[:, c, 16 + o:272 + o],
                    ALU.mult, ALU.add), fence=(c == 0), incs=[("v", 1)])
            for c in range(8):
                P.op("v", lambda v, c=c, gp=gp, o=o: v.scalar_tensor_tensor(
                    xkm2[:, gp, c, o:o + 256], d_fm[:, c, o:o + 256], negmu[:, 8 + c:9 + c],
                    ln_fm[:, c, 16 + o:272 + o],
                    ALU.mult, ALU.add), incs=[("v", 1)])
            mix_tok[(gi, hf)] = P.n("v")
            if hf == 1:
                P.op("v", lambda v: v.tensor_copy(
                    bcol, AP(ln_fm, 527, [[8 * 528, 128], [528, 8], [1, 1]])),
                    fence=True, incs=[("v", 1)])
            return P.n("v")

        def emit_body_rk(gi, ti):
            tl = groups[gi][ti]
            pp = tl % 2
            gp = gi % 2
            for which, xm, woff, bank0 in (("r", xrm2, 0, 0), ("k", xkm2, DM, 2)):
                gw = g_w[0] if which == "r" else g_w[1]
                for js in range(2):
                    bk = bank0 + js
                    gwj = gw[js] if isinstance(gw, list) else gw
                    for c in range(8):
                        w = ([("v", mix_tok[(gi, ti // 2)]), gwj] + bank_free[bk]) if c == 0 else []
                        P.op("t", lambda t, bk=bk, c=c, ti=ti, js=js, xm=xm, woff=woff, gp=gp:
                             t.matmul(pb[bk][:, :],
                                      xm[:, gp, c, 128 * ti:128 * (ti + 1)],
                                      warena[:, c, woff + 512 * js:woff + 512 * (js + 1)],
                                      start=(c == 0), stop=(c == 7)),
                             waits=w, incs=([("t", 1)] if c == 7 else []))
                    t_mm = P.n("t")
                    last_rk[gi] = t_mm
                    lastW[which] = t_mm
                    dst = r_sb if which == "r" else kpre
                    fn = AF.Sigmoid if which == "r" else AF.Relu
                    P.op("a", lambda a, bk=bk, pp=pp, js=js, dst=dst, fn=fn: a.activation(
                        dst[:, pp, 512 * js:512 * (js + 1)], pb[bk][:, :], fn),
                        waits=[("t", t_mm)] + rkt_free[pp], incs=[("s", 1)])
                    bank_free[bk] = [("s", P.n("s"))]
            s_evac = P.n("s")
            P.op("v", lambda v, pp=pp: v.tensor_tensor(
                kpre[:, pp, :], kpre[:, pp, :], kpre[:, pp, :], ALU.mult),
                waits=[("s", s_evac)] + kfm_free[pp], incs=[("v", 1)])
            sn = f"yk{pp}"
            P.op("y", YT(kfm[:, pp, :, :], kpre[:, pp, :]),
                 waits=[("v", P.n("v"))] + kfm_free[pp] +
                       ([(sn, P.n(sn))] if P.n(sn) > 0 else []),
                 incs=[(sn, 16)])
            return (sn, P.n(sn))

        def emit_body_v(gi, ti, y_k):
            tl = groups[gi][ti]
            pp = tl % 2
            for js in range(2):
                bk = 4 + js
                for c in range(8):
                    w = ([y_k, g_w[2]] + bank_free[bk]) if c == 0 else []
                    P.op("t", lambda t, bk=bk, c=c, pp=pp, js=js: t.matmul(
                        pb[bk][:, :], kfm[:, pp, c, :],
                        warena[:, c, 2 * DM + 512 * js:2 * DM + 512 * (js + 1)],
                        start=(c == 0), stop=(c == 7)),
                        waits=w, incs=([("t", 1)] if c == 7 else []))
                t_v = P.n("t")
                kfm_free[pp] = [("t", t_v)]
                lastW["v"] = t_v
                P.op("v", lambda v, bk=bk, pp=pp, js=js: v.scalar_tensor_tensor(
                    t_sb[:, pp, 512 * js:512 * (js + 1)], pb[bk][:, :], 0.5,
                    r_sb[:, pp, 512 * js:512 * (js + 1)], ALU.mult, ALU.mult),
                    waits=[("t", t_v)], incs=[("v", 1)])
                bank_free[bk] = [("v", P.n("v"))]
            P.op("v", lambda v, pp=pp, tl=tl: v.tensor_tensor(
                xr_sb[:, tl % 8, :], xr_sb[:, tl % 8, :],
                AP(t_sb, pp * DM, [[2 * DM, 128], [1, DM]]), ALU.add),
                fence=True, incs=[("v", 1)])
            xdone = P.n("v")
            rkt_free[pp] = [("v", xdone)]
            sn = f"sl{tl % 8}"
            fw = [(sn, P.n(sn))] if P.n(sn) > 0 else []
            if stage == 1:
                if STAGES == 1:
                    if OT0 <= tl <= OT1:
                        P.op("a", AD(out_d[(tl - OT0) * 128:(tl - OT0 + 1) * 128, :],
                                     xr_sb[:, tl % 8, :]),
                             waits=[("v", xdone)] + fw, incs=[(sn, 16)])
                        slot_free[tl % 8] = (sn, P.n(sn))
                    else:
                        slot_free[tl % 8] = ("v", xdone)
                else:
                    P.op("a", AD(xscr[tl * 128:(tl + 1) * 128, :], xr_sb[:, tl % 8, :]),
                         waits=[("v", xdone)] + fw, incs=[(sn, 16)])
                    slot_free[tl % 8] = (sn, P.n(sn))
                    spill_tok[tl] = (sn, P.n(sn))
            else:
                if OT0 <= tl <= OT1:
                    P.op("a", AD(out_d[(tl - OT0) * 128:(tl - OT0 + 1) * 128, :],
                                 xr_sb[:, tl % 8, :]),
                         waits=[("v", xdone)] + fw, incs=[(sn, 16)])
                    slot_free[tl % 8] = (sn, P.n(sn))
                else:
                    slot_free[tl % 8] = ("v", xdone)
            if nload[0] < len(all_tiles):
                load_x(all_tiles[nload[0]]); nload[0] += 1

        nload = [nload0]
        G = len(groups)
        for hf in range(2):
            ttok = emit_group_ln_pair(0, hf)
            emit_group_mix(0, hf, ttok)
        bnd_tok = P.n("v")
        next_ttok = {}
        pend = []
        for gi in range(G):
            for ti in range(4):
                if stage == 1 and gi == 0 and ti == 0 and STAGES > 1:
                    slot_free[0] = ("v", P.n("v"))
                    spill_tok[0] = None
                else:
                    y_k = emit_body_rk(gi, ti)
                    pend.append((gi, ti, y_k))
                if len(pend) > 1:
                    emit_body_v(*pend.pop(0))
                if gi + 1 < G and ti % 2 == 1:
                    hf = ti // 2
                    grp2 = groups[gi + 1]
                    vt = 0
                    for t2 in (2 * hf, 2 * hf + 1):
                        tl2 = grp2[t2]
                        vt = emit_ln_stats(xr_sb[:, tl2 % 8, :], t2, tl2 % 2,
                                           [xtok[tl2]])
                    stok = emit_rstd([2 * hf, 2 * hf + 1], vt)
                    tt2 = []
                    for t2 in (2 * hf, 2 * hf + 1):
                        tl3 = grp2[t2]
                        sn = f"yl{tl3 % 2}"
                        lt = emit_ln_apply(xr_sb[:, tl3 % 8, :], t2, tl3 % 2, stok)
                        P.op("y", YT(ln_fm[:, :, 16 + 128 * t2:144 + 128 * t2],
                                     ln_tm[:, tl3 % 2, :]),
                             waits=[("v", lt), ("v", bnd_tok), ("v", mix_tok[(gi, hf)])] +
                                   ([(sn, P.n(sn))] if P.n(sn) > 0 else []),
                             incs=[(sn, 16)])
                        lnT_free[tl3 % 2] = [(sn, P.n(sn))]
                        tt2.append((sn, P.n(sn)))
                    emit_group_mix(gi + 1, hf, tt2)
                    if hf == 1:
                        bnd_tok = P.n("v")
            if mid is not None and gi == 1:
                mid["tok"] = ("t", P.n("t"))
        while pend:
            emit_body_v(*pend.pop(0))
        ret = (dict(slot_free), dict(spill_tok))
        if DEBUG_DUMP and stage == 1:
            fin0 = [("t", P.n("t")), ("v", P.n("v")), ("s", P.n("s")), ("y", P.n("y"))]
            for nm, ten in (("ln_fm", ln_fm), ("xrm", xrm), ("xkm", xkm), ("r_sb", r_sb),
                            ("kpre", kpre), ("kact", kact), ("ln_tm", ln_tm),
                            ("stats", stats)):
                P.op("g", GD(dbg[nm], AP(ten, 0, [[ten.size // 128, 128],
                                                  [1, ten.size // 128]])),
                     waits=fin0, incs=[("g", 16)])
        s1.close()
        return ret

    def emit_attn_stage(barrier, t_s1_mm, t_s1, s1_slots, s1_spill):
        s2 = ExitStack()
        lsb = lambda n, s, d: s2.enter_context(nc.sbuf_tensor(n + "_s2", s, d))
        qaug = lsb("qaug", [128, 2 * DM], F16)
        kaug = lsb("kaug", [128, 2 * DM], F16)
        t1 = lsb("t1", [128, DM], F16)
        t2 = lsb("t2", [128, DM], F16)
        qpl = lsb("qpl", [128, 4, NH, 128], F16)
        kpl = lsb("kpl", [128, 4, NH, 128], F16)
        vaug = lsb("vaug", [128, 4, NH * 65], F16)
        ln2fm = lsb("ln2fm", [128, 2, 8, 128], F16)
        probsT = lsb("probsT", [128, 4, 384], BF16)
        attn_un = lsb("attn_un", [128, NH * 65], F32)
        attn_tm = lsb("attn_tm", [128, 2, DM], F16)
        attnfm = lsb("attnfm", [128, 2, 8, 128], F16)
        qm = lsb("qm", [128, 2, NH], F32)
        zs = lsb("zs", [128, 2, 2 * NH], F32)

        P.op("v", lambda v: v.memset(AP(kaug, 64, [[2 * DM, 128], [128, NH], [1, 1]]), 1.0),
             waits=list(barrier), incs=[("v", 1)])
        P.op("v", lambda v: v.memset(AP(kaug, 65, [[2 * DM, 128], [128, NH], [1, 63]]), 0.0),
             incs=[("v", 1)])
        P.op("v", lambda v: v.memset(AP(qaug, 65, [[2 * DM, 128], [128, NH], [1, 63]]), 0.0),
             incs=[("v", 1)])
        for sl in range(4):
            P.op("v", lambda v, sl=sl: v.memset(
                AP(vaug, sl * NH * 65 + 64, [[4 * NH * 65, 128], [65, NH], [1, 1]]), 1.0),
                incs=[("v", 1)])
        v_const = P.n("v")

        xtok = {}
        slot_free = dict(s1_slots or {})
        g_ffn16 = [0]
        ln2fm_free = {0: [], 1: []}
        qpl_free = {i: [] for i in range(4)}
        kpl_free = {i: [] for i in range(4)}
        vaug_free = {i: [] for i in range(4)}
        probsT_free = {i: [] for i in range(4)}
        attn_tm_free = {0: [], 1: []}
        attnfm_free = {0: [], 1: []}
        attnun_free = {0: []}
        tr_tok = {}
        v_evac = {}
        qkv_ev = {}
        wo_spill = {}
        ln2_tok = {}
        norm_tok = {}
        wo_y = {}
        exp_tok = {}
        pv_tok = {}
        sc_tok = {}
        t_qkv_last = [0]
        def load_x(tl):
            sn = f"gl{tl % 8}"
            w = [slot_free[tl % 8]] if tl % 8 in slot_free else list(barrier)
            if P.n(sn) > 0:
                w.append((sn, P.n(sn)))
            P.op("g", GD(xr_sb[:, tl % 8, :], xscr[tl * 128:(tl + 1) * 128, :]),
                 waits=w, incs=[(sn, 16)])
            xtok[tl] = (sn, P.n(sn))

        for tl in range(1, 6):
            load_x(tl)
        g_qkv = []
        for part, wkey in ((0, "r"), (1, "k"), (2, "v")):
            snq = f"awq{part}"
            P.op("a", AD(AP(warena, part * DM,
                            [[8 * 4 * DM, 128], [4 * DM, 8], [1, DM]]),
                 AP(wqkv16.tensor, part * DM,
                    [[3 * DM, 128], [128 * 3 * DM, 8], [1, DM]])),
                 waits=[("t", t_s1.get(wkey, t_s1_mm)), g_qkv16], incs=[(snq, 16)])
            g_qkv.append((snq, P.n(snq)))

        y_ln2 = {}

        def emit_ln2(j):
            par = j % 2
            sn = f"yl{par}"
            lt = emit_ln(xr_sb[:, j % 8, :], par, [xtok[j]])
            ln2_tok[j] = lt
            P.op("y", YT(ln2fm[:, par, :, :], ln_tm[:, par, :]),
                 waits=[("v", lt)] + ln2fm_free[par] +
                       ([(sn, P.n(sn))] if P.n(sn) > 0 else []),
                 incs=[(sn, 16)])
            lnT_free[par] = [(sn, P.n(sn))]
            y_ln2[j] = (sn, P.n(sn))

        def emit_qkv(j):
            par = j % 2
            y_ln = y_ln2[j]
            toks = []
            ev = []
            for js in range(6):
                bk = js % 2
                for c in range(8):
                    wm = ([y_ln, g_qkv[js // 2]] + bank_free[bk]) if c == 0 else []
                    P.op("t", lambda t, bk=bk, c=c, par=par, js=js: t.matmul(
                        pb[bk][:, :], ln2fm[:, par, c, :],
                        warena[:, c, 512 * js:512 * (js + 1)],
                        start=(c == 0), stop=False),
                        waits=wm)
                P.op("t", lambda t, bk=bk, js=js: t.matmul(
                    pb[bk][:, :], ones1[0:1, :], wqkvb_sb[0:1, 512 * js:512 * (js + 1)],
                    start=False, stop=True), incs=[("t", 1)])
                toks.append(P.n("t"))
                if js < 2:
                    dst = AP(qaug, 128 * 8 * js, [[2 * DM, 128], [128, 8], [1, 64]])
                    w = [("t", toks[js])]
                    if j - 1 in tr_tok:
                        w.append(tr_tok[j - 1])
                elif js < 4:
                    dst = AP(kaug, 128 * 8 * (js - 2), [[2 * DM, 128], [128, 8], [1, 64]])
                    w = [("t", toks[js]), ("v", v_const)]
                    if j - 1 in tr_tok:
                        w.append(tr_tok[j - 1])
                else:
                    sl = j % 4
                    dst = AP(vaug, sl * NH * 65 + 65 * 8 * (js - 4),
                             [[4 * NH * 65, 128], [65, 8], [1, 64]])
                    w = [("t", toks[js]), ("v", v_const)] + vaug_free[sl]
                P.op("a", lambda a, dst=dst, bk=bk: a.activation(dst, pb[bk][:, :], AF.Copy),
                     waits=w, incs=[("s", 1)])
                ev.append(P.n("s"))
                bank_free[bk] = [("s", P.n("s"))]
            ln2fm_free[par] = [("t", toks[-1])]
            t_qkv_last[0] = toks[-1]
            v_evac[j] = ev[5]
            qkv_ev[j] = (ev[1], ev[3])

        def emit_qkv_tail(j):
            ev1, ev3 = qkv_ev[j]
            par2 = (j + 1) % 2
            rope_tok = {}
            for which, aug, evw in (("q", qaug, ev1), ("k", kaug, ev3)):
                CC = Cq_sb if which == "q" else C_sb
                SS = Sq_sb if which == "q" else S_sb
                dv = AP(aug, 0, [[2 * DM, 128], [128, NH], [1, 64]])
                sv = AP(aug, 32, [[2 * DM, 128], [128, NH], [-32, 2], [1, 32]])
                P.op("v", lambda v, dv=dv, par2=par2, CC=CC: v.tensor_tensor(
                    t1[:, :].rearrange("p (h d) -> p h d", h=NH), dv, CC[:, par2, :]
                    .rearrange("p (h d) -> p h d", h=NH), ALU.mult),
                    waits=[("s", evw)] + ([("v", v_cq)] if which == "q" else []),
                    fence=(which == "k"), incs=[("v", 1)])
                yield
                P.op("v", lambda v, sv=sv, par2=par2, SS=SS: v.tensor_tensor(
                    t2[:, :].rearrange("p (h b d) -> p h b d", h=NH, b=2), sv,
                    SS[:, par2, :].rearrange("p (h b d) -> p h b d", h=NH, b=2), ALU.mult),
                    incs=[("v", 1)])
                P.op("v", lambda v, dv=dv: v.tensor_tensor(
                    dv, t1[:, :].rearrange("p (h d) -> p h d", h=NH),
                    t2[:, :].rearrange("p (h d) -> p h d", h=NH), ALU.add),
                    fence=True, incs=[("v", 1)])
                rope_tok[which] = P.n("v")
                yield
            P.op("a", lambda a: a.activation(qsq[:, :],
                                             AP(qaug, 0, [[2 * DM, 128], [128, NH], [1, 64]]),
                                             AF.Square),
                 waits=[("v", rope_tok["q"])] + qsq_free[0] + qsq_waw[:], incs=[("s", 1)])
            qsq_waw[:] = [("s", P.n("s"))]
            s_sq = P.n("s")
            yield
            P.op("v", lambda v: v.tensor_reduce(qm[:, 0, :],
                                                qsq[:, :].rearrange("p (h d) -> p h d", h=NH),
                                                AX.X, ALU.add),
                 waits=[("s", s_sq)], incs=[("v", 1)])
            qsq_free[0] = [("v", P.n("v"))]
            P.op("v", lambda v: v.tensor_scalar(qm[:, 0, :], qm[:, 0, :], 1e-12, None, ALU.max),
                 fence=True, incs=[("v", 1)])
            yield
            P.op("a", lambda a: a.activation(qm[:, 1, :], qm[:, 0, :], AF.Ln),
                 waits=[("v", P.n("v"))], incs=[("s", 1)])
            P.op("a", lambda a: a.activation(qm[:, 1, :], qm[:, 1, :], AF.Exp, scale=0.5),
                 waits=[("s", P.n("s"))], incs=[("s", 1)])
            s_nrm = P.n("s")
            yield
            with nc.allow_low_precision(reason="stabilizer M col; any nearby value is valid"):
                P.op("v", lambda v: v.tensor_tensor(
                    AP(qaug, 64, [[2 * DM, 128], [128, NH], [1, 1]]),
                    qm[:, 1, :], nkb2b[:, :], ALU.mult),
                    waits=[("s", s_nrm), ("v", v_pre)], incs=[("v", 1)])
            m_tok = P.n("v")
            yield
            sn = f"yt{j % 4}"
            P.op("y", YT(qpl[:, j % 4, :, :], qaug[:, :]),
                 waits=[("v", m_tok)] + qpl_free[j % 4] +
                       ([(sn, P.n(sn))] if P.n(sn) > 0 else []),
                 incs=[(sn, 16)])
            P.op("y", YT(kpl[:, j % 4, :, :], kaug[:, :]),
                 waits=[("v", rope_tok["k"])] + kpl_free[j % 4], incs=[(sn, 16)])
            tr_tok[j] = (sn, P.n(sn))

        def emit_ln2_steps(j):
            par = j % 2
            sl = lambda c: stats[:, par, c:c + 1]
            P.op("a", lambda a: a.activation(ln_tm[:, par, :], xr_sb[:, j % 8, :], AF.Copy,
                                             accum_out=sl(0)),
                 waits=[xtok[j]] + lnT_free[par] + stats_free.pop(par, []),
                 incs=[("s", 1)])
            P.op("a", lambda a: a.activation(qsq[:, :], xr_sb[:, j % 8, :], AF.Square,
                                             accum_out=sl(1)),
                 waits=list(qsq_free[0]) + qsq_waw[:], incs=[("s", 1)])
            qsq_waw[:] = [("s", P.n("s"))]
            s_sq = P.n("s")
            yield
            P.op("v", lambda v: v.tensor_scalar(sl(2), sl(0), 1.0 / DM, None, ALU.mult),
                 waits=[("s", s_sq)], incs=[("v", 1)])
            P.op("v", lambda v: v.tensor_tensor(sl(3), sl(2), sl(2), ALU.mult), fence=True,
                 incs=[("v", 1)])
            yield
            P.op("v", lambda v: v.scalar_tensor_tensor(sl(4), sl(1), 1.0 / DM, sl(3),
                                                       ALU.mult, ALU.subtract),
                 fence=True, incs=[("v", 1)])
            P.op("v", lambda v: v.tensor_scalar(sl(4), sl(4), 1e-5, None, ALU.add),
                 fence=True, incs=[("v", 1)])
            wtok = P.n("v")
            yield
            stok = emit_rstd([par], wtok)
            yield
            lt = emit_ln_apply(xr_sb[:, j % 8, :], par, par, stok)
            ln2_tok[j] = lt
            yield
            sn = f"yl{par}"
            P.op("y", YT(ln2fm[:, par, :, :], ln_tm[:, par, :]),
                 waits=[("v", lt)] + ln2fm_free[par] +
                       ([(sn, P.n(sn))] if P.n(sn) > 0 else []),
                 incs=[(sn, 16)])
            lnT_free[par] = [(sn, P.n(sn))]
            y_ln2[j] = (sn, P.n(sn))

        def emit_attn_core(i, feeder=None):
            par = i % 2
            win = [(i - 1) % 4, i % 4, (i + 1) % 4]
            feed = (lambda: next(feeder, None)) if feeder is not None else (lambda: None)
            gi = 1 if i == 3 else (2 if i == 18 else 0)
            LAG = 2
            for hh in range(NH + LAG):
                if hh < NH:
                    h = hh
                    hp = h % 4
                    bk_sc = 2 if h % 2 == 0 else 7
                    w0 = [tr_tok[i - 1], tr_tok[i], tr_tok[i + 1], ("s", v_evac[i + 1])] \
                        if h == 0 else []
                    if (i, h - 2) in exp_tok:
                        w0 = w0 + [("s", exp_tok[(i, h - 2)])]
                    elif h < 2 and (i - 1, NH - 2 + h) in exp_tok:
                        w0 = w0 + [("s", exp_tok[(i - 1, NH - 2 + h)])]
                    for ci, sl in enumerate(win):
                        w = (w0 + bank_free[bk_sc]) if ci == 0 else []
                        P.op("t", lambda t, sl=sl, h=h, ci=ci, par=par, bk=bk_sc: t.matmul(
                            pb[bk][:, ci * 128:(ci + 1) * 128],
                            AP(kpl, sl * NH * 128 + h * 128, [[4 * NH * 128, 65], [1, 128]]),
                            AP(qpl, (i % 4) * NH * 128 + h * 128, [[4 * NH * 128, 65], [1, 128]]),
                            start=True, stop=True),
                            waits=w, incs=([("t", 1)] if ci == 2 else []))
                    sc_tok[(i, h)] = P.n("t")
                    bank_free[bk_sc] = []
                    wp = probsT_free[hp]
                    P.op("a", lambda a, h=h, bk=bk_sc, hp=hp: a.activation(
                        probsT[:, hp, :], pb[bk][:, 0:384], AF.Exp),
                        waits=[("t", sc_tok[(i, h)])] + wp, incs=[("s", 1)])
                    exp_tok[(i, h)] = P.n("s")
                    bank_free[bk_sc] = [("s", exp_tok[(i, h)])]
                    P.op("v", lambda v, hp=hp, gi=gi: v.tensor_tensor(
                        AP(probsT, hp * 384, [[4 * 384, 128], [256, 2], [1, 128]]),
                        AP(probsT, hp * 384, [[4 * 384, 128], [256, 2], [1, 128]]),
                        gates_sb[:, gi, :].rearrange("p (b d) -> p b d", b=2), ALU.mult),
                        waits=[("s", exp_tok[(i, h)]), a_misc], incs=[("v", 1)])
                    exp_tok[(i, h, "g")] = P.n("v")
                h2 = hh - LAG
                if h2 >= 0:
                    h = h2
                    hp = h % 4
                    bk_pv = 3 if h % 2 == 0 else 6
                    w = [("v", exp_tok[(i, h, "g")])] + bank_free[bk_pv]
                    for ci, sl in enumerate(win):
                        P.op("t", lambda t, sl=sl, h=h, ci=ci, hp=hp, bk=bk_pv: t.matmul(
                            pb[bk][:, 0:65],
                            AP(probsT, hp * 384 + ci * 128, [[4 * 384, 128], [1, 128]]),
                            AP(vaug, sl * NH * 65 + h * 65, [[4 * NH * 65, 128], [1, 65]]),
                            start=(ci == 0), stop=(ci == 2)),
                            waits=(w if ci == 0 else []), incs=([("t", 1)] if ci == 2 else []))
                    pv_tok[(i, h)] = P.n("t")
                    probsT_free[hp] = [("t", pv_tok[(i, h)])]
                    wv = attnun_free[0] if h <= 1 else []
                    if h % 2 == 0:
                        P.op("a", lambda a, h=h, bk=bk_pv: a.activation(
                            attn_un[:, h * 65:(h + 1) * 65], pb[bk][:, 0:65], AF.Copy),
                            waits=[("t", pv_tok[(i, h)])] + wv, incs=[("s", 1)])
                        bank_free[bk_pv] = [("s", P.n("s"))]
                    else:
                        P.op("v", lambda v, h=h, bk=bk_pv: v.tensor_copy(
                            attn_un[:, h * 65:(h + 1) * 65], pb[bk][:, 0:65]),
                            waits=[("t", pv_tok[(i, h)])] + wv, incs=[("v", 1)])
                        bank_free[bk_pv] = [("v", P.n("v"))]
                feed()
            t_last = pv_tok[(i, NH - 1)]
            for sl in win:
                kpl_free[sl] = [("t", sc_tok[(i, NH - 1)])]
                vaug_free[sl] = [("t", t_last)]
            qpl_free[i % 4] = [("t", sc_tok[(i, NH - 1)])]
            P.op("v", lambda v, par=par: v.tensor_scalar(
                zs[:, par, 0:NH], AP(attn_un, 64, [[NH * 65, 128], [65, NH], [1, 1]]),
                1e-30, None, ALU.max),
                waits=[("s", P.n("s")), ("v", P.n("v"))], incs=[("v", 1)])
            P.op("v", lambda v, par=par: v.reciprocal(zs[:, par, NH:2 * NH], zs[:, par, 0:NH]),
                 fence=True, incs=[("v", 1)])
            for h in range(NH):
                P.op("v", lambda v, h=h, par=par: v.tensor_scalar(
                    attn_tm[:, par, h * 64:(h + 1) * 64],
                    attn_un[:, h * 65:h * 65 + 64],
                    zs[:, par, NH + h:NH + h + 1], None, ALU.mult),
                    waits=(attn_tm_free[par] if h == 0 else []), fence=(h == 0),
                    incs=([("v", 1)] if h == NH - 1 else []))
            norm_tok[i] = P.n("v")
            attnun_free[0] = [("v", norm_tok[i])]
            sn = f"yaf{par}"
            P.op("y", YT(attnfm[:, par, :, :], attn_tm[:, par, :]),
                 waits=[("v", norm_tok[i])] + attnfm_free[par] +
                       ([(sn, P.n(sn))] if P.n(sn) > 0 else []),
                 incs=[(sn, 16)])
            attn_tm_free[par] = [(sn, P.n(sn))]
            wo_y[i] = (sn, P.n(sn))

        def emit_wo(i):
            par = i % 2
            for js in range(2):
                bk = 4 + js
                for c in range(8):
                    w = ([wo_y[i], g_wo] + bank_free[bk]) if c == 0 else []
                    P.op("t", lambda t, bk=bk, c=c, par=par, js=js: t.matmul(
                        pb[bk][:, :], attnfm[:, par, c, :],
                        warena[:, c, 3 * DM + 512 * js:3 * DM + 512 * (js + 1)],
                        start=(c == 0), stop=False),
                        waits=w)
                P.op("t", lambda t, bk=bk, js=js: t.matmul(
                    pb[bk][:, :], ones1[0:1, :], wob_sb[0:1, 512 * js:512 * (js + 1)],
                    start=False, stop=True), incs=[("t", 1)])
                t_wo = P.n("t")
                if js == 1:
                    attnfm_free[par] = [("t", t_wo)]
                P.op("v", lambda v, bk=bk, i=i, js=js: v.tensor_tensor(
                    AP(xr_sb, (i % 8) * DM + 512 * js, [[8 * DM, 128], [1, 512]]),
                    pb[bk][:, :],
                    AP(xr_sb, (i % 8) * DM + 512 * js, [[8 * DM, 128], [1, 512]]),
                    ALU.add),
                    waits=[("t", t_wo), ("v", ln2_tok[i])], incs=[("v", 1)])
                bank_free[bk] = [("v", P.n("v"))]
            x2_tok = P.n("v")
            sn = f"gl{i % 8}"
            fw = [(sn, P.n(sn))] if P.n(sn) > 0 else []
            if STAGES == 3 and OT0 <= i <= OT1:
                P.op("g", GD(out_d[(i - OT0) * 128:(i - OT0 + 1) * 128, :], xr_sb[:, i % 8, :]),
                     waits=[("v", x2_tok)] + fw, incs=[(sn, 16)])
            else:
                P.op("g", GD(xscr[i * 128:(i + 1) * 128, :], xr_sb[:, i % 8, :]),
                     waits=[("v", x2_tok)] + fw, incs=[(sn, 16)])
            slot_free[i % 8] = (sn, P.n(sn))
            wo_spill[i] = (sn, P.n(sn))

        emit_ln2(1)
        for j in range(1, NT):
            emit_qkv(j)

            def mk_feeder(j=j):
                yield from emit_qkv_tail(j)
                if j + 1 < NT:
                    yield from emit_ln2_steps(j + 1)
            fd = mk_feeder()
            if QT0 <= j - 2 <= QT1:
                emit_attn_core(j - 2, feeder=fd)
            for _ in fd:
                pass
            if j == 1 or j == 19:
                slot_free[j % 8] = ("v", ln2_tok[j])
            if QT0 <= j - 3 <= QT1:
                emit_wo(j - 3)
            if j == 6:
                for ii, wd in enumerate((wr_d, wk_d, wv_d)):
                    P.op("g", GD(ffn16[:, ii * DM:(ii + 1) * DM], wd),
                         waits=([("t", P.n("t"))] if ii == 0 else []), incs=[("gf6", 16)])
                g_ffn16[0] = P.n("gf6")
            if j + 5 < NT:
                load_x(j + 5)
        g_w4 = []
        for sn4, i in (("aw4r", 0), ("aw4k", 1), ("aw4v", 2)):
            P.op("a", AD(warena[:, :, i * DM:(i + 1) * DM],
                         ffn16[:, i * DM:(i + 1) * DM].rearrange("(c p) d -> p c d", p=128)),
                 waits=([("t", t_qkv_last[0]), ("gf6", g_ffn16[0])] if i == 0 else []),
                 incs=[(sn4, 16)])
            g_w4.append((sn4, P.n(sn4)))
        emit_attn_core(18)
        emit_wo(17)
        emit_wo(18)
        s2.close()
        return g_w4, dict(slot_free), dict(wo_spill)

    s1_mid = {}
    s1_lastW = {}
    s1_slots, s1_spill = emit_ffn_stage(1, barrier=[], g_w=g_ffnw, mid=s1_mid, lastW=s1_lastW)
    t_s1_mm = P.n("t")
    P.op("g", GD(wqkv16[:, :], wqkv_d), waits=[s1_mid.get("tok", ("t", 0))],
         incs=[("gq6", 16)])
    g_qkv16 = ("gq6", P.n("gq6"))
    mkbar = lambda: ([("t", P.n("t")), ("v", P.n("v")), ("s", P.n("s"))] +
                     [(nm, P.n(nm)) for nm in STREAMS if P.n(nm) > 0])
    if STAGES > 1:
        bar1 = mkbar()
        g_w4, s23_slots, s23_spill = emit_attn_stage(bar1, t_s1_mm, s1_lastW, s1_slots, s1_spill)
        if STAGES > 3:
            bar2 = mkbar()
            emit_ffn_stage(4, barrier=bar2, g_w=g_w4, slot_init=s23_slots, spill_prev=s23_spill)

    fin = [(nm, P.n(nm)) for nm in
           (["a", "y", "t", "v", "s", "g",
             "gm", "gw0", "gw1", "gw2", "gwo", "gq6", "gf6"] + STREAMS)
           if P.n(nm) > 0]
    P.op("g", lambda g: None, waits=fin)

    with nc.Block() as block, ExitStack() as semstack:
        sems = {}
        for s in P.cnt:
            sems[s] = semstack.enter_context(nc.semaphore(s))

        @block.tensor
        def _(t):
            P.replay("t", t, sems)

        @block.vector
        def _(v):
            P.replay("v", v, sems)

        @block.scalar
        def _(a):
            P.replay("a", a, sems)

        @block.gpsimd
        def _(g):
            P.replay("g", g, sems)

        @block.sync
        def _(y):
            P.replay("y", y, sems)

    st.close()
    return nc


def host_prep(inputs):
    x = np.ascontiguousarray(inputs["L0_attn_hidden_states"], dtype=np.float32)
    wqkv = np.ascontiguousarray(inputs["L0_attn_Wqkv_kernel"], dtype=np.float32)
    wqkvb = np.ascontiguousarray(inputs["L0_attn_Wqkv_bias"], dtype=np.float32)
    wo = np.ascontiguousarray(inputs["L0_attn_Wo_kernel"], dtype=np.float32)
    wob = np.ascontiguousarray(inputs["L0_attn_Wo_bias"], dtype=np.float32)
    rope = np.ascontiguousarray(inputs["L0_attn_rope_cache"], dtype=np.float32)
    wr = np.ascontiguousarray(inputs["L0_ffn_W_r"], dtype=np.float32).T.copy()
    wk = np.ascontiguousarray(inputs["L0_ffn_W_k"], dtype=np.float32).T.copy()
    wv = np.ascontiguousarray(inputs["L0_ffn_W_v"], dtype=np.float32).T.copy()
    mu_r = np.ascontiguousarray(inputs["L0_ffn_mu_r"], dtype=np.float32)
    mu_k = np.ascontiguousarray(inputs["L0_ffn_mu_k"], dtype=np.float32)

    perm64 = np.concatenate([np.arange(0, 64, 2), np.arange(1, 64, 2)])
    permh = np.concatenate([h * 64 + perm64 for h in range(NH)])
    colperm = np.concatenate([permh, 1024 + permh, np.arange(2048, 3072)])
    wqkv_p = wqkv[:, colperm].copy()
    wqkvb_p = wqkvb[colperm][None, :].copy()

    mu = np.concatenate([mu_r.reshape(8, 128).T, mu_k.reshape(8, 128).T], axis=1).copy()

    cos = rope[:, :, 0]
    sin = rope[:, :, 1]
    cs_head = np.concatenate([cos, cos], axis=1)
    ss_head = np.concatenate([sin, sin], axis=1)
    C = np.tile(cs_head, (1, NH)).reshape(2, 128, 2 * 512)
    S = np.tile(ss_head, (1, NH)).reshape(2, 128, 2 * 512)
    cosx = np.concatenate([C[0], C[1]], axis=1).copy()
    sinx = np.concatenate([S[0], S[1]], axis=1).copy()
    pm1 = np.tile(np.concatenate([-np.ones(32, np.float32), np.ones(32, np.float32)]), NH)
    pm1 = np.tile(pm1[None, :], (128, 1)).astype(ml_dtypes.float16 if False else np.float16)

    i = np.arange(128)
    gA = (i[:, None] >= i[None, :]).astype(np.float32)
    gC = (i[:, None] <= i[None, :]).astype(np.float32)
    interior = np.concatenate([gA, gC], axis=1)
    first = np.concatenate([np.zeros_like(gA), gC], axis=1)
    last = np.concatenate([gA, np.zeros_like(gC)], axis=1)

    in_maps = []
    for core in range(8):
        b, h = core // 2, core % 2
        S0 = h * 2048
        lo, hi = S0 - 384, S0 + 2176
        lo_c, hi_c = max(lo, 0), min(hi, 4096)
        xs = np.zeros((TW, DM), np.float32)
        xs[lo_c - lo:hi_c - lo] = x[b, lo_c:hi_c]
        g1 = first if h == 0 else interior
        g2 = interior if h == 0 else last
        gates = np.stack([interior, g1, g2], axis=1).astype(ml_dtypes.bfloat16)
        bsel = np.full((128, 1), float(h), np.float32)
        in_maps.append(dict(
            x0=xs, wqkv=wqkv_p, wqkvb=wqkvb_p, wo=wo, wob=wob[None, :].copy(),
            wr=wr, wk=wk, wv=wv, mu=mu, cosx=cosx, sinx=sinx, pm1=pm1,
            gates=np.ascontiguousarray(gates), bsel=bsel))
    return in_maps


_NC_CACHE = None


def kernel(**inputs):
    global _NC_CACHE
    in_maps = host_prep(inputs)
    if _NC_CACHE is None:
        _NC_CACHE = build_nc()
    res = run_bass_kernel_spmd(_NC_CACHE, in_maps, core_ids=list(range(8)))
    out = np.empty((4, 4096, DM), np.float32)
    for core in range(8):
        b, h = core // 2, core % 2
        out[b, h * 2048:(h + 1) * 2048] = res.results[core]["out"]
    return out


# version 29 (final)
# speedup vs baseline: 1.1230x; 1.0357x over previous
import numpy as np
import ml_dtypes
import concourse.bass as bass
import concourse.mybir as mybir
from concourse.bass_utils import run_bass_kernel_spmd
from contextlib import ExitStack

F32 = mybir.dt.float32
F16 = mybir.dt.float16
BF16 = mybir.dt.bfloat16
AF = mybir.ActivationFunctionType
ALU = mybir.AluOpType
AX = mybir.AxisListType
AP = bass.AP

NH, HD, DM = 16, 64, 1024
NT = 20
TW = NT * 128
QT0, QT1 = 2, 18
OT0, OT1 = 3, 18
DEBUG_DUMP = False
STAGES = 4


class Prog:
    def __init__(self, nc):
        self.nc = nc
        self.qs = {k: [] for k in "tvagy"}
        self.cnt = {}

    def add(self, name):
        self.cnt[name] = 0

    def op(self, q, build, waits=(), incs=(), fence=False):
        waits = [(s, v) for (s, v) in waits if v > 0]
        if fence and self.qs[q]:
            prev = self.qs[q][-1]
            if prev[2]:
                sem, amt, cum = prev[2][0]
                waits.append((sem, cum))
            else:
                fs = "f" + q
                self.cnt[fs] += 1
                prev[2].append((fs, 1, self.cnt[fs]))
                waits.append((fs, self.cnt[fs]))
        rec = []
        for s, a in incs:
            self.cnt[s] += a
            rec.append((s, a, self.cnt[s]))
        self.qs[q].append([waits, build, rec])

    def n(self, sem):
        return self.cnt[sem]

    def replay(self, q, eng, sems):
        for waits, build, incs in self.qs[q]:
            for s, v in waits:
                eng.wait_ge(sems[s], v)
            inst = build(eng)
            for s, a, _ in incs:
                if inst is None:
                    eng.sem_inc(sems[s], a)
                else:
                    inst.then_inc(sems[s], a)


def build_nc():
    nc = bass.Bass("TRN2", target_bir_lowering=False, debug=False)
    dt = nc.dram_tensor
    x0_d = dt("x0", [TW, DM], F32, kind="ExternalInput").ap()
    wqkv_d = dt("wqkv", [DM, 3 * DM], F32, kind="ExternalInput").ap()
    wqkvb_d = dt("wqkvb", [1, 3 * DM], F32, kind="ExternalInput").ap()
    wo_d = dt("wo", [DM, DM], F32, kind="ExternalInput").ap()
    wob_d = dt("wob", [1, DM], F32, kind="ExternalInput").ap()
    wr_d = dt("wr", [DM, DM], F32, kind="ExternalInput").ap()
    wk_d = dt("wk", [DM, DM], F32, kind="ExternalInput").ap()
    wv_d = dt("wv", [DM, DM], F32, kind="ExternalInput").ap()
    mu_d = dt("mu", [128, 16], F32, kind="ExternalInput").ap()
    cos_d = dt("cosx", [128, 2 * DM], F32, kind="ExternalInput").ap()
    sin_d = dt("sinx", [128, 2 * DM], F32, kind="ExternalInput").ap()
    pm1_d = dt("pm1", [128, DM], F16, kind="ExternalInput").ap()
    gates_d = dt("gates", [128, 3, 256], BF16, kind="ExternalInput").ap()
    bsel_d = dt("bsel", [128, 1], F32, kind="ExternalInput").ap()
    nout = (OT1 - OT0 + 1) * 128
    out_d = dt("out", [nout, DM], F32, kind="ExternalOutput").ap()
    xscr = dt("xscr", [TW, DM], F32).ap()
    wqkv16 = dt("wqkv16", [DM, 3 * DM], F16).ap()
    ffn16 = dt("ffn16", [DM, 3 * DM], F16).ap()
    dbg = {}
    if DEBUG_DUMP:
        for nm, shape in (("ln_fm", [128, 8 * 528]), ("xrm", [128, 8 * 512]),
                          ("xkm", [128, 8 * 512]), ("r_sb", [128, 2 * DM]),
                          ("kpre", [128, 2 * DM]), ("kact", [128, 2 * DM]),
                          ("ln_tm", [128, 2 * DM]), ("stats", [128, 16])):
            dbg[nm] = dt("dbg_" + nm, shape, F32, kind="ExternalOutput").ap()

    st = ExitStack()
    sb = lambda n, s, d: st.enter_context(nc.sbuf_tensor(n, s, d))

    xr_sb = sb("xr_sb", [128, 8, DM], F32)
    warena = sb("warena", [128, 8, 4 * DM], F16)
    C_sb = sb("C_sb", [128, 2, DM], F16)
    S_sb = sb("S_sb", [128, 2, DM], F16)
    Cq_sb = sb("Cq_sb", [128, 2, DM], F16)
    Sq_sb = sb("Sq_sb", [128, 2, DM], F16)
    pm1_sb = sb("pm1_sb", [128, DM], F16)
    gates_sb = sb("gates_sb", [128, 3, 256], BF16)
    mu_sb = sb("mu_sb", [128, 16], F32)
    negmu = sb("negmu", [128, 16], F32)
    wqkvb_sb = sb("wqkvb_sb", [1, 3 * DM], F16)
    wob_sb = sb("wob_sb", [1, DM], F16)
    ones1 = sb("ones1", [1, 128], F16)
    bsel_sb = sb("bsel_sb", [128, 1], F32)
    kb2s = sb("kb2s", [1, 2 * NH], F32)
    kb2row = sb("kb2row", [1, NH], F16)
    nkb2b = sb("nkb2b", [128, NH], F32)
    bsq = sb("bsq", [1, DM], F32)
    ln_tm = sb("ln_tm", [128, 2, DM], F16)
    stats = sb("stats", [128, 4, 8], F32)
    qsq = sb("qsq", [128, DM], F16)
    qsq_free = {0: []}
    qsq_waw = []
    stats_free = {}
    pb = [st.enter_context(nc.psum_tensor(f"pb{i}", [128, 512], F32)) for i in range(8)]

    P = Prog(nc)
    STREAMS = (["am", "awq0", "awq1", "awq2", "aw4r", "aw4r2", "aw4k", "aw4v", "ylb"] +
               [f"sl{i}" for i in range(8)] + [f"gl{i}" for i in range(8)] +
               ["yl0", "yl1", "yk0", "yk1", "yt0", "yt1", "yt2", "yt3", "yaf0", "yaf1",
                "gw0a", "gw0b"])
    for s in (["g", "a", "y", "t", "v", "s", "ft", "fv", "fa", "fg", "fy",
               "gm", "gw0", "gw1", "gw2", "gwo", "gq6", "gf6"] + STREAMS):
        P.add(s)

    GD = lambda o, i: (lambda g: g.dma_start(o, i))
    AD = lambda o, i: (lambda a: a.dma_start(o, i))
    YT = lambda o, i: (lambda y: y.dma_start(o, i, transpose=True))

    bank_free = {i: [] for i in range(8)}
    lnT_free = {0: [], 1: []}

    g_ffnw3 = []

    def wchunks(wd, base, n0, n1, sem):
        for off in range(n0, n1, 256):
            P.op("g", GD(warena[:, :, base + off:base + off + 256],
                         AP(wd.tensor, off, [[DM, 128], [128 * DM, 8], [1, 256]])),
                 incs=[(sem, 16)])
    wchunks(wr_d, 0, 0, 512, "gw0a")
    wchunks(wr_d, 0, 512, 1024, "gw0b")
    g_ffnw3.append([("gw0a", 32), ("gw0b", 32)])
    for o, i in ((wqkvb_sb[:, :], wqkvb_d), (wob_sb[:, :], wob_d), (C_sb[:, :, :], cos_d),
                 (S_sb[:, :, :], sin_d), (pm1_sb[:, :], pm1_d)):
        P.op("g", GD(o, i), incs=[("gm", 16)])
    g_misc = ("gm", P.n("gm"))
    for nmw, (i, wd) in (("gw1", (1, wk_d)), ("gw2", (2, wv_d))):
        wchunks(wd, i * DM, 0, DM, nmw)
        g_ffnw3.append((nmw, P.n(nmw)))
    g_ffnw = list(g_ffnw3)
    wchunks(wo_d, 3 * DM, 0, DM, "gwo")
    g_wo = ("gwo", P.n("gwo"))
    for o, i in ((gates_sb[:, :, :], gates_d), (mu_sb[:, :], mu_d), (bsel_sb[:, :], bsel_d)):
        P.op("a", AD(o, i), incs=[("am", 16)])
    a_misc = ("am", P.n("am"))

    P.op("v", lambda v: v.memset(ones1[:, :], 1.0), incs=[("v", 1)])
    P.op("v", lambda v: v.tensor_scalar(negmu[:, :], mu_sb[:, :], 1.0, None, ALU.subtract),
         waits=[a_misc], incs=[("v", 1)])
    for par in range(2):
        P.op("v", lambda v, par=par: v.tensor_tensor(S_sb[:, par, :], S_sb[:, par, :],
                                                     pm1_sb[:, :], ALU.mult),
             waits=[g_misc], incs=[("v", 1)])
    P.op("v", lambda v: v.tensor_scalar(Cq_sb[:, :, :], C_sb[:, :, :], 0.125, None, ALU.mult),
         waits=[g_misc], incs=[("v", 1)])
    P.op("v", lambda v: v.tensor_scalar(Sq_sb[:, :, :], S_sb[:, :, :], 0.125, None, ALU.mult),
         fence=True, incs=[("v", 1)])
    v_cq = P.n("v")
    P.op("a", lambda a: a.activation(bsq[:, :], wqkvb_sb[:, DM:2 * DM], AF.Square),
         waits=[g_misc, ("v", P.n("v"))], incs=[("s", 1)])
    P.op("v", lambda v: v.tensor_reduce(kb2s[:, 0:NH],
                                        bsq[:, :].rearrange("p (h d) -> p h d", h=NH),
                                        AX.X, ALU.add),
         waits=[("s", P.n("s"))], incs=[("v", 1)])
    P.op("v", lambda v: v.tensor_scalar(kb2s[:, 0:NH], kb2s[:, 0:NH], 1e-12, None, ALU.max),
         fence=True, incs=[("v", 1)])
    P.op("a", lambda a: a.activation(kb2s[:, NH:2 * NH], kb2s[:, 0:NH], AF.Ln),
         waits=[("v", P.n("v"))], incs=[("s", 1)])
    P.op("a", lambda a: a.activation(kb2s[:, NH:2 * NH], kb2s[:, NH:2 * NH], AF.Exp, scale=0.5),
         waits=[("s", P.n("s"))], incs=[("s", 1)])
    with nc.allow_low_precision(reason="stabilizer scale; any nearby value is valid"):
        P.op("v", lambda v: v.tensor_scalar(kb2row[:, :], kb2s[:, NH:2 * NH], 14.2, -1.0,
                                            ALU.add, ALU.mult),
             waits=[("s", P.n("s"))], incs=[("v", 1)])
    P.op("t", lambda t: t.matmul(pb[7][:, 0:NH], ones1[0:1, :], kb2row[0:1, :],
                                 start=True, stop=True),
         waits=[("v", P.n("v"))], incs=[("t", 1)])
    P.op("v", lambda v: v.tensor_copy(nkb2b[:, :], pb[7][:, 0:NH]),
         waits=[("t", P.n("t"))], incs=[("v", 1)])
    P.op("v", lambda v: v.tensor_scalar(nkb2b[:, :], nkb2b[:, :], 0.125, None, ALU.mult),
         fence=True, incs=[("v", 1)])
    bank_free[7] = [("v", P.n("v"))]
    v_pre = P.n("v")

    def emit_ln_stats(x_ap, slot, par, waits):
        sl = lambda c: stats[:, slot, c:c + 1]
        P.op("a", lambda a: a.activation(ln_tm[:, par, :], x_ap, AF.Copy, accum_out=sl(0)),
             waits=list(waits) + lnT_free[par] + stats_free.pop(slot, []),
             incs=[("s", 1)])
        P.op("a", lambda a: a.activation(qsq[:, :], x_ap, AF.Square, accum_out=sl(1)),
             waits=list(qsq_free[0]) + qsq_waw[:], incs=[("s", 1)])
        qsq_waw[:] = [("s", P.n("s"))]
        s_sq = P.n("s")
        P.op("v", lambda v: v.tensor_scalar(sl(2), sl(0), 1.0 / DM, None, ALU.mult),
             waits=[("s", s_sq)], incs=[("v", 1)])
        P.op("v", lambda v: v.tensor_tensor(sl(3), sl(2), sl(2), ALU.mult), fence=True,
             incs=[("v", 1)])
        P.op("v", lambda v: v.scalar_tensor_tensor(sl(4), sl(1), 1.0 / DM, sl(3),
                                                   ALU.mult, ALU.subtract),
             fence=True, incs=[("v", 1)])
        P.op("v", lambda v: v.tensor_scalar(sl(4), sl(4), 1e-5, None, ALU.add),
             fence=True, incs=[("v", 1)])
        return P.n("v")

    def emit_rstd(slots, wtok):
        lo, n = min(slots), len(slots)
        lnv = AP(stats, lo * 8 + 6, [[4 * 8, 128], [8, n], [1, 1]])
        var = AP(stats, lo * 8 + 4, [[4 * 8, 128], [8, n], [1, 1]])
        rst = AP(stats, lo * 8 + 5, [[4 * 8, 128], [8, n], [1, 1]])
        P.op("a", lambda a: a.activation(lnv, var, AF.Ln), waits=[("v", wtok)], incs=[("s", 1)])
        P.op("a", lambda a: a.activation(rst, lnv, AF.Exp, scale=-0.5),
             waits=[("s", P.n("s"))], incs=[("s", 1)])
        return P.n("s")

    def emit_ln_apply(x_ap, slot, par, stok, waits=()):
        sl = lambda c: stats[:, slot, c:c + 1]
        P.op("v", lambda v: v.tensor_scalar(ln_tm[:, par, :], x_ap, sl(2), sl(5),
                                            ALU.subtract, ALU.mult),
             waits=[("s", stok)] + list(waits) + lnT_free[par], incs=[("v", 1)])
        stats_free[slot] = [("v", P.n("v"))]
        return P.n("v")

    def emit_ln(x_ap, par, waits):
        wtok = emit_ln_stats(x_ap, par, par, waits)
        stok = emit_rstd([par], wtok)
        return emit_ln_apply(x_ap, par, par, stok)

    def emit_ffn_stage(stage, barrier, g_w, mid=None, lastW=None, slot_init=None, spill_prev=None):
        if lastW is None:
            lastW = {}
        s1 = ExitStack()
        lsb = lambda n, s, d: s1.enter_context(nc.sbuf_tensor(n + f"_s{stage}", s, d))
        ln_fm = lsb("ln_fm", [128, 8, 528], F16)
        d_fm = lsb("d_fm", [128, 8, 512], F16)
        xrm2 = lsb("xrm2", [128, 2, 8, 512], F16)
        xkm2 = lsb("xkm2", [128, 2, 8, 512], F16)
        r_sb = lsb("r_sb", [128, 2, DM], F16)
        kpre = lsb("kpre", [128, 2, DM], F16)
        kfm = lsb("kfm", [128, 2, 8, 128], F16)
        t_sb = lsb("t_sb", [128, 2, DM], F32)

        if stage == 1:
            src = x0_d
            groups = [list(range(4 * g, 4 * g + 4)) for g in range(5)]
            bnd = None
        else:
            src = xscr
            groups = [[3, 4, 5, 6], [7, 8, 9, 10], [11, 12, 13, 14], [15, 16, 17, 18]]
            bnd = 2
        all_tiles = ([bnd] if bnd is not None else []) + [t for g in groups for t in g]

        xtok = {}
        slot_free = dict(slot_init or {})
        spill_tok = {}
        kfm_free = {0: [], 1: []}
        rkt_free = {0: [], 1: []}

        qd, pfx = ("a", "sl") if stage == 1 else ("g", "gl")

        def load_x(tl):
            sn = f"{pfx}{tl % 8}"
            w = ([slot_free[tl % 8]] if tl % 8 in slot_free else list(barrier))
            if P.n(sn) > 0:
                w.append((sn, P.n(sn)))
            if spill_prev is not None and spill_prev.get(tl):
                w.append(spill_prev[tl])
            P.op(qd, (AD if qd == "a" else GD)(xr_sb[:, tl % 8, :],
                                               src[tl * 128:(tl + 1) * 128, :]),
                 waits=w, incs=[(sn, 16)])
            xtok[tl] = (sn, P.n(sn))

        nload0 = 0
        for tl in all_tiles[:8]:
            load_x(tl); nload0 += 1

        bcol = AP(ln_fm, 15, [[8 * 528, 128], [528, 8], [1, 1]])
        if bnd is not None:
            lnb = lsb("lnb", [128, 8, 128], F16)
            lt = emit_ln(xr_sb[:, bnd % 8, :], bnd % 2, [xtok[bnd]])
            P.op("y", YT(lnb[:, :, :], ln_tm[:, bnd % 2, :]), waits=[("v", lt)],
                 incs=[("ylb", 16)])
            lnT_free[bnd % 2] = [("ylb", P.n("ylb"))]
            P.op("v", lambda v: v.tensor_scalar(
                bcol, AP(lnb, 127, [[8 * 128, 128], [128, 8], [1, 1]]),
                bsel_sb[:, 0:1], None, ALU.mult),
                waits=[("ylb", P.n("ylb")), a_misc], incs=[("v", 1)])
            slot_free[bnd % 8] = ("v", P.n("v"))
        else:
            P.op("v", lambda v: v.memset(bcol, 0.0), waits=list(barrier), incs=[("v", 1)])
        bnd_tok = P.n("v")
        mix_tok = {}
        last_rk = {}

        def emit_group_ln_pair(gi, hf):
            grp = groups[gi]
            vt = 0
            for ti in (2 * hf, 2 * hf + 1):
                tl = grp[ti]
                vt = emit_ln_stats(xr_sb[:, tl % 8, :], ti, tl % 2, [xtok[tl]])
            stok = emit_rstd([2 * hf, 2 * hf + 1], vt)
            ttok = []
            for ti in (2 * hf, 2 * hf + 1):
                tl = grp[ti]
                sn = f"yl{tl % 2}"
                lt = emit_ln_apply(xr_sb[:, tl % 8, :], ti, tl % 2, stok)
                P.op("y", YT(ln_fm[:, :, 16 + 128 * ti:144 + 128 * ti], ln_tm[:, tl % 2, :]),
                     waits=[("v", lt), ("v", bnd_tok)] +
                           ([(sn, P.n(sn))] if P.n(sn) > 0 else []) +
                           ([("v", mix_tok[(gi - 1, hf)])] if (gi - 1, hf) in mix_tok else []),
                     incs=[(sn, 16)])
                lnT_free[tl % 2] = [(sn, P.n(sn))]
                ttok.append((sn, P.n(sn)))
            return ttok

        def emit_group_mix(gi, hf, ttok):
            gp = gi % 2
            o = 256 * hf
            w = list(ttok) + [a_misc] + ([("t", last_rk[gi - 2])] if gi - 2 in last_rk else [])
            P.op("v", lambda v, o=o: v.tensor_tensor(
                d_fm[:, :, o:o + 256], ln_fm[:, :, 16 + o:272 + o],
                ln_fm[:, :, 15 + o:271 + o], ALU.subtract),
                waits=w, incs=[("v", 1)])
            for c in range(8):
                P.op("v", lambda v, c=c, gp=gp, o=o: v.scalar_tensor_tensor(
                    xrm2[:, gp, c, o:o + 256], d_fm[:, c, o:o + 256], negmu[:, c:c + 1],
                    ln_fm[:, c, 16 + o:272 + o],
                    ALU.mult, ALU.add), fence=(c == 0), incs=[("v", 1)])
            for c in range(8):
                P.op("v", lambda v, c=c, gp=gp, o=o: v.scalar_tensor_tensor(
                    xkm2[:, gp, c, o:o + 256], d_fm[:, c, o:o + 256], negmu[:, 8 + c:9 + c],
                    ln_fm[:, c, 16 + o:272 + o],
                    ALU.mult, ALU.add), incs=[("v", 1)])
            mix_tok[(gi, hf)] = P.n("v")
            if hf == 1:
                P.op("v", lambda v: v.tensor_copy(
                    bcol, AP(ln_fm, 527, [[8 * 528, 128], [528, 8], [1, 1]])),
                    fence=True, incs=[("v", 1)])
            return P.n("v")

        def emit_body_rk(gi, ti):
            tl = groups[gi][ti]
            pp = tl % 2
            gp = gi % 2
            for which, xm, woff, bank0 in (("r", xrm2, 0, 0), ("k", xkm2, DM, 2)):
                gw = g_w[0] if which == "r" else g_w[1]
                for js in range(2):
                    bk = bank0 + js
                    gwj = gw[js] if isinstance(gw, list) else gw
                    for c in range(8):
                        w = ([("v", mix_tok[(gi, ti // 2)]), gwj] + bank_free[bk]) if c == 0 else []
                        P.op("t", lambda t, bk=bk, c=c, ti=ti, js=js, xm=xm, woff=woff, gp=gp:
                             t.matmul(pb[bk][:, :],
                                      xm[:, gp, c, 128 * ti:128 * (ti + 1)],
                                      warena[:, c, woff + 512 * js:woff + 512 * (js + 1)],
                                      start=(c == 0), stop=(c == 7)),
                             waits=w, incs=([("t", 1)] if c == 7 else []))
                    t_mm = P.n("t")
                    last_rk[gi] = t_mm
                    lastW[which] = t_mm
                    dst = r_sb if which == "r" else kpre
                    fn = AF.Sigmoid if which == "r" else AF.Relu
                    P.op("a", lambda a, bk=bk, pp=pp, js=js, dst=dst, fn=fn: a.activation(
                        dst[:, pp, 512 * js:512 * (js + 1)], pb[bk][:, :], fn),
                        waits=[("t", t_mm)] + rkt_free[pp], incs=[("s", 1)])
                    bank_free[bk] = [("s", P.n("s"))]
            s_evac = P.n("s")
            P.op("v", lambda v, pp=pp: v.tensor_tensor(
                kpre[:, pp, :], kpre[:, pp, :], kpre[:, pp, :], ALU.mult),
                waits=[("s", s_evac)] + kfm_free[pp], incs=[("v", 1)])
            sn = f"yk{pp}"
            P.op("y", YT(kfm[:, pp, :, :], kpre[:, pp, :]),
                 waits=[("v", P.n("v"))] + kfm_free[pp] +
                       ([(sn, P.n(sn))] if P.n(sn) > 0 else []),
                 incs=[(sn, 16)])
            return (sn, P.n(sn))

        def emit_body_v(gi, ti, y_k):
            tl = groups[gi][ti]
            pp = tl % 2
            for js in range(2):
                bk = 4 + js
                for c in range(8):
                    w = ([y_k, g_w[2]] + bank_free[bk]) if c == 0 else []
                    P.op("t", lambda t, bk=bk, c=c, pp=pp, js=js: t.matmul(
                        pb[bk][:, :], kfm[:, pp, c, :],
                        warena[:, c, 2 * DM + 512 * js:2 * DM + 512 * (js + 1)],
                        start=(c == 0), stop=(c == 7)),
                        waits=w, incs=([("t", 1)] if c == 7 else []))
                t_v = P.n("t")
                kfm_free[pp] = [("t", t_v)]
                lastW["v"] = t_v
                P.op("v", lambda v, bk=bk, pp=pp, js=js: v.scalar_tensor_tensor(
                    t_sb[:, pp, 512 * js:512 * (js + 1)], pb[bk][:, :], 0.5,
                    r_sb[:, pp, 512 * js:512 * (js + 1)], ALU.mult, ALU.mult),
                    waits=[("t", t_v)], incs=[("v", 1)])
                bank_free[bk] = [("v", P.n("v"))]
            P.op("v", lambda v, pp=pp, tl=tl: v.tensor_tensor(
                xr_sb[:, tl % 8, :], xr_sb[:, tl % 8, :],
                AP(t_sb, pp * DM, [[2 * DM, 128], [1, DM]]), ALU.add),
                fence=True, incs=[("v", 1)])
            xdone = P.n("v")
            rkt_free[pp] = [("v", xdone)]
            sn = f"sl{tl % 8}"
            fw = [(sn, P.n(sn))] if P.n(sn) > 0 else []
            if stage == 1:
                if STAGES == 1:
                    if OT0 <= tl <= OT1:
                        P.op("a", AD(out_d[(tl - OT0) * 128:(tl - OT0 + 1) * 128, :],
                                     xr_sb[:, tl % 8, :]),
                             waits=[("v", xdone)] + fw, incs=[(sn, 16)])
                        slot_free[tl % 8] = (sn, P.n(sn))
                    else:
                        slot_free[tl % 8] = ("v", xdone)
                else:
                    P.op("a", AD(xscr[tl * 128:(tl + 1) * 128, :], xr_sb[:, tl % 8, :]),
                         waits=[("v", xdone)] + fw, incs=[(sn, 16)])
                    slot_free[tl % 8] = (sn, P.n(sn))
                    spill_tok[tl] = (sn, P.n(sn))
            else:
                if OT0 <= tl <= OT1:
                    P.op("a", AD(out_d[(tl - OT0) * 128:(tl - OT0 + 1) * 128, :],
                                 xr_sb[:, tl % 8, :]),
                         waits=[("v", xdone)] + fw, incs=[(sn, 16)])
                    slot_free[tl % 8] = (sn, P.n(sn))
                else:
                    slot_free[tl % 8] = ("v", xdone)
            if nload[0] < len(all_tiles):
                load_x(all_tiles[nload[0]]); nload[0] += 1

        nload = [nload0]
        G = len(groups)
        for hf in range(2):
            ttok = emit_group_ln_pair(0, hf)
            emit_group_mix(0, hf, ttok)
        bnd_tok = P.n("v")
        next_ttok = {}
        pend = []
        for gi in range(G):
            for ti in range(4):
                if stage == 1 and gi == 0 and ti == 0 and STAGES > 1:
                    slot_free[0] = ("v", P.n("v"))
                    spill_tok[0] = None
                else:
                    y_k = emit_body_rk(gi, ti)
                    pend.append((gi, ti, y_k))
                if len(pend) > 1:
                    emit_body_v(*pend.pop(0))
                if gi + 1 < G and ti % 2 == 1:
                    hf = ti // 2
                    grp2 = groups[gi + 1]
                    vt = 0
                    for t2 in (2 * hf, 2 * hf + 1):
                        tl2 = grp2[t2]
                        vt = emit_ln_stats(xr_sb[:, tl2 % 8, :], t2, tl2 % 2,
                                           [xtok[tl2]])
                    stok = emit_rstd([2 * hf, 2 * hf + 1], vt)
                    tt2 = []
                    for t2 in (2 * hf, 2 * hf + 1):
                        tl3 = grp2[t2]
                        sn = f"yl{tl3 % 2}"
                        lt = emit_ln_apply(xr_sb[:, tl3 % 8, :], t2, tl3 % 2, stok)
                        P.op("y", YT(ln_fm[:, :, 16 + 128 * t2:144 + 128 * t2],
                                     ln_tm[:, tl3 % 2, :]),
                             waits=[("v", lt), ("v", bnd_tok), ("v", mix_tok[(gi, hf)])] +
                                   ([(sn, P.n(sn))] if P.n(sn) > 0 else []),
                             incs=[(sn, 16)])
                        lnT_free[tl3 % 2] = [(sn, P.n(sn))]
                        tt2.append((sn, P.n(sn)))
                    emit_group_mix(gi + 1, hf, tt2)
                    if hf == 1:
                        bnd_tok = P.n("v")
            if mid is not None and gi >= 1:
                mid.setdefault("toks", []).append(("t", P.n("t")))
        while pend:
            emit_body_v(*pend.pop(0))
        ret = (dict(slot_free), dict(spill_tok))
        if DEBUG_DUMP and stage == 1:
            fin0 = [("t", P.n("t")), ("v", P.n("v")), ("s", P.n("s")), ("y", P.n("y"))]
            for nm, ten in (("ln_fm", ln_fm), ("xrm", xrm), ("xkm", xkm), ("r_sb", r_sb),
                            ("kpre", kpre), ("kact", kact), ("ln_tm", ln_tm),
                            ("stats", stats)):
                P.op("g", GD(dbg[nm], AP(ten, 0, [[ten.size // 128, 128],
                                                  [1, ten.size // 128]])),
                     waits=fin0, incs=[("g", 16)])
        s1.close()
        return ret

    def emit_attn_stage(barrier, t_s1_mm, t_s1, s1_slots, s1_spill):
        s2 = ExitStack()
        lsb = lambda n, s, d: s2.enter_context(nc.sbuf_tensor(n + "_s2", s, d))
        qaug = lsb("qaug", [128, 2 * DM], F16)
        kaug = lsb("kaug", [128, 2 * DM], F16)
        t1 = lsb("t1", [128, DM], F16)
        t2 = lsb("t2", [128, DM], F16)
        qpl = lsb("qpl", [128, 4, NH, 128], F16)
        kpl = lsb("kpl", [128, 4, NH, 128], F16)
        vaug = lsb("vaug", [128, 4, NH * 65], F16)
        ln2fm = lsb("ln2fm", [128, 2, 8, 128], F16)
        probsT = lsb("probsT", [128, 4, 384], BF16)
        attn_un = lsb("attn_un", [128, NH * 65], F32)
        attn_tm = lsb("attn_tm", [128, 2, DM], F16)
        attnfm = lsb("attnfm", [128, 2, 8, 128], F16)
        qm = lsb("qm", [128, 2, NH], F32)
        zs = lsb("zs", [128, 2, 2 * NH], F32)

        P.op("v", lambda v: v.memset(AP(kaug, 64, [[2 * DM, 128], [128, NH], [1, 1]]), 1.0),
             waits=list(barrier), incs=[("v", 1)])
        P.op("v", lambda v: v.memset(AP(kaug, 65, [[2 * DM, 128], [128, NH], [1, 63]]), 0.0),
             incs=[("v", 1)])
        P.op("v", lambda v: v.memset(AP(qaug, 65, [[2 * DM, 128], [128, NH], [1, 63]]), 0.0),
             incs=[("v", 1)])
        for sl in range(4):
            P.op("v", lambda v, sl=sl: v.memset(
                AP(vaug, sl * NH * 65 + 64, [[4 * NH * 65, 128], [65, NH], [1, 1]]), 1.0),
                incs=[("v", 1)])
        v_const = P.n("v")

        xtok = {}
        slot_free = dict(s1_slots or {})
        g_ffn16 = [0]
        ln2fm_free = {0: [], 1: []}
        qpl_free = {i: [] for i in range(4)}
        kpl_free = {i: [] for i in range(4)}
        vaug_free = {i: [] for i in range(4)}
        probsT_free = {i: [] for i in range(4)}
        attn_tm_free = {0: [], 1: []}
        attnfm_free = {0: [], 1: []}
        attnun_free = {0: []}
        tr_tok = {}
        v_evac = {}
        qkv_ev = {}
        wo_spill = {}
        ln2_tok = {}
        norm_tok = {}
        wo_y = {}
        exp_tok = {}
        pv_tok = {}
        sc_tok = {}
        t_qkv_last = [0]
        t_qkv_js = {}
        def load_x(tl):
            sn = f"gl{tl % 8}"
            w = [slot_free[tl % 8]] if tl % 8 in slot_free else list(barrier)
            if P.n(sn) > 0:
                w.append((sn, P.n(sn)))
            P.op("g", GD(xr_sb[:, tl % 8, :], xscr[tl * 128:(tl + 1) * 128, :]),
                 waits=w, incs=[(sn, 16)])
            xtok[tl] = (sn, P.n(sn))

        for tl in range(1, 6):
            load_x(tl)
        g_qkv = []
        for part, wkey in ((0, "r"), (1, "k"), (2, "v")):
            snq = f"awq{part}"
            for hf in range(2):
                P.op("a", AD(AP(warena, part * DM + 512 * hf,
                                [[8 * 4 * DM, 128], [4 * DM, 8], [1, 512]]),
                     AP(wqkv16.tensor, part * DM + 512 * hf,
                        [[3 * DM, 128], [128 * 3 * DM, 8], [1, 512]])),
                     waits=([("t", t_s1.get(wkey, t_s1_mm)), g_qkv16] if hf == 0 else []),
                     incs=[(snq, 16)])
            g_qkv.append((snq, P.n(snq)))

        y_ln2 = {}

        def emit_ln2(j):
            par = j % 2
            sn = f"yl{par}"
            lt = emit_ln(xr_sb[:, j % 8, :], par, [xtok[j]])
            ln2_tok[j] = lt
            P.op("y", YT(ln2fm[:, par, :, :], ln_tm[:, par, :]),
                 waits=[("v", lt)] + ln2fm_free[par] +
                       ([(sn, P.n(sn))] if P.n(sn) > 0 else []),
                 incs=[(sn, 16)])
            lnT_free[par] = [(sn, P.n(sn))]
            y_ln2[j] = (sn, P.n(sn))

        def emit_qkv(j):
            par = j % 2
            y_ln = y_ln2[j]
            toks = []
            ev = []
            for js in range(6):
                bk = js % 2
                for c in range(8):
                    wm = ([y_ln, g_qkv[js // 2]] + bank_free[bk]) if c == 0 else []
                    P.op("t", lambda t, bk=bk, c=c, par=par, js=js: t.matmul(
                        pb[bk][:, :], ln2fm[:, par, c, :],
                        warena[:, c, 512 * js:512 * (js + 1)],
                        start=(c == 0), stop=False),
                        waits=wm)
                P.op("t", lambda t, bk=bk, js=js: t.matmul(
                    pb[bk][:, :], ones1[0:1, :], wqkvb_sb[0:1, 512 * js:512 * (js + 1)],
                    start=False, stop=True), incs=[("t", 1)])
                toks.append(P.n("t"))
                if js < 2:
                    dst = AP(qaug, 128 * 8 * js, [[2 * DM, 128], [128, 8], [1, 64]])
                    w = [("t", toks[js])]
                    if j - 1 in tr_tok:
                        w.append(tr_tok[j - 1])
                elif js < 4:
                    dst = AP(kaug, 128 * 8 * (js - 2), [[2 * DM, 128], [128, 8], [1, 64]])
                    w = [("t", toks[js]), ("v", v_const)]
                    if j - 1 in tr_tok:
                        w.append(tr_tok[j - 1])
                else:
                    sl = j % 4
                    dst = AP(vaug, sl * NH * 65 + 65 * 8 * (js - 4),
                             [[4 * NH * 65, 128], [65, 8], [1, 64]])
                    w = [("t", toks[js]), ("v", v_const)] + vaug_free[sl]
                P.op("a", lambda a, dst=dst, bk=bk: a.activation(dst, pb[bk][:, :], AF.Copy),
                     waits=w, incs=[("s", 1)])
                ev.append(P.n("s"))
                bank_free[bk] = [("s", P.n("s"))]
            ln2fm_free[par] = [("t", toks[-1])]
            t_qkv_last[0] = toks[-1]
            if j == NT - 1:
                t_qkv_js.update({0: toks[1], 1: toks[3], 2: toks[5]})
            v_evac[j] = ev[5]
            qkv_ev[j] = (ev[1], ev[3])

        def emit_qkv_tail(j):
            ev1, ev3 = qkv_ev[j]
            par2 = (j + 1) % 2
            rope_tok = {}
            for which, aug, evw in (("q", qaug, ev1), ("k", kaug, ev3)):
                CC = Cq_sb if which == "q" else C_sb
                SS = Sq_sb if which == "q" else S_sb
                dv = AP(aug, 0, [[2 * DM, 128], [128, NH], [1, 64]])
                sv = AP(aug, 32, [[2 * DM, 128], [128, NH], [-32, 2], [1, 32]])
                P.op("v", lambda v, dv=dv, par2=par2, CC=CC: v.tensor_tensor(
                    t1[:, :].rearrange("p (h d) -> p h d", h=NH), dv, CC[:, par2, :]
                    .rearrange("p (h d) -> p h d", h=NH), ALU.mult),
                    waits=[("s", evw)] + ([("v", v_cq)] if which == "q" else []),
                    fence=(which == "k"), incs=[("v", 1)])
                yield
                P.op("v", lambda v, sv=sv, par2=par2, SS=SS: v.tensor_tensor(
                    t2[:, :].rearrange("p (h b d) -> p h b d", h=NH, b=2), sv,
                    SS[:, par2, :].rearrange("p (h b d) -> p h b d", h=NH, b=2), ALU.mult),
                    incs=[("v", 1)])
                P.op("v", lambda v, dv=dv: v.tensor_tensor(
                    dv, t1[:, :].rearrange("p (h d) -> p h d", h=NH),
                    t2[:, :].rearrange("p (h d) -> p h d", h=NH), ALU.add),
                    fence=True, incs=[("v", 1)])
                rope_tok[which] = P.n("v")
                yield
            P.op("a", lambda a: a.activation(qsq[:, :],
                                             AP(qaug, 0, [[2 * DM, 128], [128, NH], [1, 64]]),
                                             AF.Square),
                 waits=[("v", rope_tok["q"])] + qsq_free[0] + qsq_waw[:], incs=[("s", 1)])
            qsq_waw[:] = [("s", P.n("s"))]
            s_sq = P.n("s")
            yield
            P.op("v", lambda v: v.tensor_reduce(qm[:, 0, :],
                                                qsq[:, :].rearrange("p (h d) -> p h d", h=NH),
                                                AX.X, ALU.add),
                 waits=[("s", s_sq)], incs=[("v", 1)])
            qsq_free[0] = [("v", P.n("v"))]
            P.op("v", lambda v: v.tensor_scalar(qm[:, 0, :], qm[:, 0, :], 1e-12, None, ALU.max),
                 fence=True, incs=[("v", 1)])
            yield
            P.op("a", lambda a: a.activation(qm[:, 1, :], qm[:, 0, :], AF.Ln),
                 waits=[("v", P.n("v"))], incs=[("s", 1)])
            P.op("a", lambda a: a.activation(qm[:, 1, :], qm[:, 1, :], AF.Exp, scale=0.5),
                 waits=[("s", P.n("s"))], incs=[("s", 1)])
            s_nrm = P.n("s")
            yield
            with nc.allow_low_precision(reason="stabilizer M col; any nearby value is valid"):
                P.op("v", lambda v: v.tensor_tensor(
                    AP(qaug, 64, [[2 * DM, 128], [128, NH], [1, 1]]),
                    qm[:, 1, :], nkb2b[:, :], ALU.mult),
                    waits=[("s", s_nrm), ("v", v_pre)], incs=[("v", 1)])
            m_tok = P.n("v")
            yield
            sn = f"yt{j % 4}"
            P.op("y", YT(qpl[:, j % 4, :, :], qaug[:, :]),
                 waits=[("v", m_tok)] + qpl_free[j % 4] +
                       ([(sn, P.n(sn))] if P.n(sn) > 0 else []),
                 incs=[(sn, 16)])
            P.op("y", YT(kpl[:, j % 4, :, :], kaug[:, :]),
                 waits=[("v", rope_tok["k"])] + kpl_free[j % 4], incs=[(sn, 16)])
            tr_tok[j] = (sn, P.n(sn))

        def emit_ln2_steps(j):
            par = j % 2
            sl = lambda c: stats[:, par, c:c + 1]
            P.op("a", lambda a: a.activation(ln_tm[:, par, :], xr_sb[:, j % 8, :], AF.Copy,
                                             accum_out=sl(0)),
                 waits=[xtok[j]] + lnT_free[par] + stats_free.pop(par, []),
                 incs=[("s", 1)])
            P.op("a", lambda a: a.activation(qsq[:, :], xr_sb[:, j % 8, :], AF.Square,
                                             accum_out=sl(1)),
                 waits=list(qsq_free[0]) + qsq_waw[:], incs=[("s", 1)])
            qsq_waw[:] = [("s", P.n("s"))]
            s_sq = P.n("s")
            yield
            P.op("v", lambda v: v.tensor_scalar(sl(2), sl(0), 1.0 / DM, None, ALU.mult),
                 waits=[("s", s_sq)], incs=[("v", 1)])
            P.op("v", lambda v: v.tensor_tensor(sl(3), sl(2), sl(2), ALU.mult), fence=True,
                 incs=[("v", 1)])
            yield
            P.op("v", lambda v: v.scalar_tensor_tensor(sl(4), sl(1), 1.0 / DM, sl(3),
                                                       ALU.mult, ALU.subtract),
                 fence=True, incs=[("v", 1)])
            P.op("v", lambda v: v.tensor_scalar(sl(4), sl(4), 1e-5, None, ALU.add),
                 fence=True, incs=[("v", 1)])
            wtok = P.n("v")
            yield
            stok = emit_rstd([par], wtok)
            yield
            lt = emit_ln_apply(xr_sb[:, j % 8, :], par, par, stok)
            ln2_tok[j] = lt
            yield
            sn = f"yl{par}"
            P.op("y", YT(ln2fm[:, par, :, :], ln_tm[:, par, :]),
                 waits=[("v", lt)] + ln2fm_free[par] +
                       ([(sn, P.n(sn))] if P.n(sn) > 0 else []),
                 incs=[(sn, 16)])
            lnT_free[par] = [(sn, P.n(sn))]
            y_ln2[j] = (sn, P.n(sn))

        def emit_attn_core(i, feeder=None):
            par = i % 2
            win = [(i - 1) % 4, i % 4, (i + 1) % 4]
            feed = (lambda: next(feeder, None)) if feeder is not None else (lambda: None)
            gi = 1 if i == 3 else (2 if i == 18 else 0)
            LAG = 2
            for hh in range(NH + LAG):
                if hh < NH:
                    h = hh
                    hp = h % 4
                    bk_sc = 2 if h % 2 == 0 else 7
                    w0 = [tr_tok[i - 1], tr_tok[i], tr_tok[i + 1], ("s", v_evac[i + 1])] \
                        if h == 0 else []
                    if (i, h - 2) in exp_tok:
                        w0 = w0 + [("s", exp_tok[(i, h - 2)])]
                    elif h < 2 and (i - 1, NH - 2 + h) in exp_tok:
                        w0 = w0 + [("s", exp_tok[(i - 1, NH - 2 + h)])]
                    for ci, sl in enumerate(win):
                        w = (w0 + bank_free[bk_sc]) if ci == 0 else []
                        P.op("t", lambda t, sl=sl, h=h, ci=ci, par=par, bk=bk_sc: t.matmul(
                            pb[bk][:, ci * 128:(ci + 1) * 128],
                            AP(kpl, sl * NH * 128 + h * 128, [[4 * NH * 128, 65], [1, 128]]),
                            AP(qpl, (i % 4) * NH * 128 + h * 128, [[4 * NH * 128, 65], [1, 128]]),
                            start=True, stop=True),
                            waits=w, incs=([("t", 1)] if ci == 2 else []))
                    sc_tok[(i, h)] = P.n("t")
                    bank_free[bk_sc] = []
                    wp = probsT_free[hp]
                    P.op("a", lambda a, h=h, bk=bk_sc, hp=hp: a.activation(
                        probsT[:, hp, :], pb[bk][:, 0:384], AF.Exp),
                        waits=[("t", sc_tok[(i, h)])] + wp, incs=[("s", 1)])
                    exp_tok[(i, h)] = P.n("s")
                    bank_free[bk_sc] = [("s", exp_tok[(i, h)])]
                    P.op("v", lambda v, hp=hp, gi=gi: v.tensor_tensor(
                        AP(probsT, hp * 384, [[4 * 384, 128], [256, 2], [1, 128]]),
                        AP(probsT, hp * 384, [[4 * 384, 128], [256, 2], [1, 128]]),
                        gates_sb[:, gi, :].rearrange("p (b d) -> p b d", b=2), ALU.mult),
                        waits=[("s", exp_tok[(i, h)]), a_misc], incs=[("v", 1)])
                    exp_tok[(i, h, "g")] = P.n("v")
                h2 = hh - LAG
                if h2 >= 0:
                    h = h2
                    hp = h % 4
                    bk_pv = 3 if h % 2 == 0 else 6
                    w = [("v", exp_tok[(i, h, "g")])] + bank_free[bk_pv]
                    for ci, sl in enumerate(win):
                        P.op("t", lambda t, sl=sl, h=h, ci=ci, hp=hp, bk=bk_pv: t.matmul(
                            pb[bk][:, 0:65],
                            AP(probsT, hp * 384 + ci * 128, [[4 * 384, 128], [1, 128]]),
                            AP(vaug, sl * NH * 65 + h * 65, [[4 * NH * 65, 128], [1, 65]]),
                            start=(ci == 0), stop=(ci == 2)),
                            waits=(w if ci == 0 else []), incs=([("t", 1)] if ci == 2 else []))
                    pv_tok[(i, h)] = P.n("t")
                    probsT_free[hp] = [("t", pv_tok[(i, h)])]
                    wv = attnun_free[0] if h <= 1 else []
                    if h % 2 == 0:
                        P.op("a", lambda a, h=h, bk=bk_pv: a.activation(
                            attn_un[:, h * 65:(h + 1) * 65], pb[bk][:, 0:65], AF.Copy),
                            waits=[("t", pv_tok[(i, h)])] + wv, incs=[("s", 1)])
                        bank_free[bk_pv] = [("s", P.n("s"))]
                    else:
                        P.op("v", lambda v, h=h, bk=bk_pv: v.tensor_copy(
                            attn_un[:, h * 65:(h + 1) * 65], pb[bk][:, 0:65]),
                            waits=[("t", pv_tok[(i, h)])] + wv, incs=[("v", 1)])
                        bank_free[bk_pv] = [("v", P.n("v"))]
                feed()
            t_last = pv_tok[(i, NH - 1)]
            for sl in win:
                kpl_free[sl] = [("t", sc_tok[(i, NH - 1)])]
                vaug_free[sl] = [("t", t_last)]
            qpl_free[i % 4] = [("t", sc_tok[(i, NH - 1)])]
            P.op("v", lambda v, par=par: v.tensor_scalar(
                zs[:, par, 0:NH], AP(attn_un, 64, [[NH * 65, 128], [65, NH], [1, 1]]),
                1e-30, None, ALU.max),
                waits=[("s", P.n("s")), ("v", P.n("v"))], incs=[("v", 1)])
            P.op("v", lambda v, par=par: v.reciprocal(zs[:, par, NH:2 * NH], zs[:, par, 0:NH]),
                 fence=True, incs=[("v", 1)])
            for h in range(NH):
                P.op("v", lambda v, h=h, par=par: v.tensor_scalar(
                    attn_tm[:, par, h * 64:(h + 1) * 64],
                    attn_un[:, h * 65:h * 65 + 64],
                    zs[:, par, NH + h:NH + h + 1], None, ALU.mult),
                    waits=(attn_tm_free[par] if h == 0 else []), fence=(h == 0),
                    incs=([("v", 1)] if h == NH - 1 else []))
            norm_tok[i] = P.n("v")
            attnun_free[0] = [("v", norm_tok[i])]
            sn = f"yaf{par}"
            P.op("y", YT(attnfm[:, par, :, :], attn_tm[:, par, :]),
                 waits=[("v", norm_tok[i])] + attnfm_free[par] +
                       ([(sn, P.n(sn))] if P.n(sn) > 0 else []),
                 incs=[(sn, 16)])
            attn_tm_free[par] = [(sn, P.n(sn))]
            wo_y[i] = (sn, P.n(sn))

        def emit_wo(i):
            par = i % 2
            for js in range(2):
                bk = 4 + js
                for c in range(8):
                    w = ([wo_y[i], g_wo] + bank_free[bk]) if c == 0 else []
                    P.op("t", lambda t, bk=bk, c=c, par=par, js=js: t.matmul(
                        pb[bk][:, :], attnfm[:, par, c, :],
                        warena[:, c, 3 * DM + 512 * js:3 * DM + 512 * (js + 1)],
                        start=(c == 0), stop=False),
                        waits=w)
                P.op("t", lambda t, bk=bk, js=js: t.matmul(
                    pb[bk][:, :], ones1[0:1, :], wob_sb[0:1, 512 * js:512 * (js + 1)],
                    start=False, stop=True), incs=[("t", 1)])
                t_wo = P.n("t")
                if js == 1:
                    attnfm_free[par] = [("t", t_wo)]
                P.op("v", lambda v, bk=bk, i=i, js=js: v.tensor_tensor(
                    AP(xr_sb, (i % 8) * DM + 512 * js, [[8 * DM, 128], [1, 512]]),
                    pb[bk][:, :],
                    AP(xr_sb, (i % 8) * DM + 512 * js, [[8 * DM, 128], [1, 512]]),
                    ALU.add),
                    waits=[("t", t_wo), ("v", ln2_tok[i])], incs=[("v", 1)])
                bank_free[bk] = [("v", P.n("v"))]
            x2_tok = P.n("v")
            sn = f"gl{i % 8}"
            fw = [(sn, P.n(sn))] if P.n(sn) > 0 else []
            if STAGES == 3 and OT0 <= i <= OT1:
                P.op("g", GD(out_d[(i - OT0) * 128:(i - OT0 + 1) * 128, :], xr_sb[:, i % 8, :]),
                     waits=[("v", x2_tok)] + fw, incs=[(sn, 16)])
            else:
                P.op("g", GD(xscr[i * 128:(i + 1) * 128, :], xr_sb[:, i % 8, :]),
                     waits=[("v", x2_tok)] + fw, incs=[(sn, 16)])
            slot_free[i % 8] = (sn, P.n(sn))
            wo_spill[i] = (sn, P.n(sn))

        emit_ln2(1)
        for j in range(1, NT):
            emit_qkv(j)

            def mk_feeder(j=j):
                yield from emit_qkv_tail(j)
                if j + 1 < NT:
                    yield from emit_ln2_steps(j + 1)
            fd = mk_feeder()
            if QT0 <= j - 2 <= QT1:
                emit_attn_core(j - 2, feeder=fd)
            for _ in fd:
                pass
            if j == 1 or j == 19:
                slot_free[j % 8] = ("v", ln2_tok[j])
            if QT0 <= j - 3 <= QT1:
                emit_wo(j - 3)
            if 6 <= j <= 11:
                cc = j - 6
                wd = (wr_d, wk_d, wv_d)[cc // 2]
                off = 512 * (cc % 2)
                P.op("g", GD(ffn16[:, cc * 512:cc * 512 + 512],
                             AP(wd.tensor, off, [[DM, DM], [1, 512]])),
                     waits=[("t", P.n("t"))], incs=[("gf6", 16)])
                g_ffn16[0] = P.n("gf6")
            if j + 5 < NT:
                load_x(j + 5)
        rhalf = []
        for hf, sn4 in ((0, "aw4r"), (1, "aw4r2")):
            P.op("a", AD(AP(warena, 512 * hf, [[8 * 4 * DM, 128], [4 * DM, 8], [1, 512]]),
                         AP(ffn16.tensor, 512 * hf,
                            [[3 * DM, 128], [128 * 3 * DM, 8], [1, 512]])),
                 waits=([("t", t_qkv_js.get(0, t_qkv_last[0])), ("gf6", g_ffn16[0])]
                        if hf == 0 else []),
                 incs=[(sn4, 16)])
            rhalf.append((sn4, P.n(sn4)))
        g_w4 = [rhalf]
        for sn4, i in (("aw4k", 1), ("aw4v", 2)):
            P.op("a", AD(warena[:, :, i * DM:(i + 1) * DM],
                         ffn16[:, i * DM:(i + 1) * DM].rearrange("(c p) d -> p c d", p=128)),
                 waits=[("t", t_qkv_js.get(i, t_qkv_last[0]))],
                 incs=[(sn4, 16)])
            g_w4.append((sn4, P.n(sn4)))
        emit_attn_core(18)
        emit_wo(17)
        emit_wo(18)
        s2.close()
        return g_w4, dict(slot_free), dict(wo_spill)

    s1_mid = {}
    s1_lastW = {}
    s1_slots, s1_spill = emit_ffn_stage(1, barrier=[], g_w=g_ffnw, mid=s1_mid, lastW=s1_lastW)
    t_s1_mm = P.n("t")
    mtoks = s1_mid.get("toks", [("t", 0)])
    for cc in range(6):
        off = 512 * cc
        P.op("g", GD(wqkv16[:, off:off + 512],
                     AP(wqkv_d.tensor, off, [[3 * DM, DM], [1, 512]])),
             waits=[mtoks[min(cc // 2, len(mtoks) - 1)]], incs=[("gq6", 16)])
    g_qkv16 = ("gq6", P.n("gq6"))
    mkbar = lambda: ([("t", P.n("t")), ("v", P.n("v")), ("s", P.n("s"))] +
                     [(nm, P.n(nm)) for nm in STREAMS if P.n(nm) > 0])
    if STAGES > 1:
        bar1 = mkbar()
        g_w4, s23_slots, s23_spill = emit_attn_stage(bar1, t_s1_mm, s1_lastW, s1_slots, s1_spill)
        if STAGES > 3:
            bar2 = mkbar()
            emit_ffn_stage(4, barrier=bar2, g_w=g_w4, slot_init=s23_slots, spill_prev=s23_spill)

    fin = [(nm, P.n(nm)) for nm in
           (["a", "y", "t", "v", "s", "g",
             "gm", "gw0", "gw1", "gw2", "gwo", "gq6", "gf6"] + STREAMS)
           if P.n(nm) > 0]
    P.op("g", lambda g: None, waits=fin)

    with nc.Block() as block, ExitStack() as semstack:
        sems = {}
        for s in P.cnt:
            sems[s] = semstack.enter_context(nc.semaphore(s))

        @block.tensor
        def _(t):
            P.replay("t", t, sems)

        @block.vector
        def _(v):
            P.replay("v", v, sems)

        @block.scalar
        def _(a):
            P.replay("a", a, sems)

        @block.gpsimd
        def _(g):
            P.replay("g", g, sems)

        @block.sync
        def _(y):
            P.replay("y", y, sems)

    st.close()
    return nc


def host_prep(inputs):
    x = np.ascontiguousarray(inputs["L0_attn_hidden_states"], dtype=np.float32)
    wqkv = np.ascontiguousarray(inputs["L0_attn_Wqkv_kernel"], dtype=np.float32)
    wqkvb = np.ascontiguousarray(inputs["L0_attn_Wqkv_bias"], dtype=np.float32)
    wo = np.ascontiguousarray(inputs["L0_attn_Wo_kernel"], dtype=np.float32)
    wob = np.ascontiguousarray(inputs["L0_attn_Wo_bias"], dtype=np.float32)
    rope = np.ascontiguousarray(inputs["L0_attn_rope_cache"], dtype=np.float32)
    wr = np.ascontiguousarray(inputs["L0_ffn_W_r"], dtype=np.float32).T.copy()
    wk = np.ascontiguousarray(inputs["L0_ffn_W_k"], dtype=np.float32).T.copy()
    wv = np.ascontiguousarray(inputs["L0_ffn_W_v"], dtype=np.float32).T.copy()
    mu_r = np.ascontiguousarray(inputs["L0_ffn_mu_r"], dtype=np.float32)
    mu_k = np.ascontiguousarray(inputs["L0_ffn_mu_k"], dtype=np.float32)

    perm64 = np.concatenate([np.arange(0, 64, 2), np.arange(1, 64, 2)])
    permh = np.concatenate([h * 64 + perm64 for h in range(NH)])
    colperm = np.concatenate([permh, 1024 + permh, np.arange(2048, 3072)])
    wqkv_p = wqkv[:, colperm].copy()
    wqkvb_p = wqkvb[colperm][None, :].copy()

    mu = np.concatenate([mu_r.reshape(8, 128).T, mu_k.reshape(8, 128).T], axis=1).copy()

    cos = rope[:, :, 0]
    sin = rope[:, :, 1]
    cs_head = np.concatenate([cos, cos], axis=1)
    ss_head = np.concatenate([sin, sin], axis=1)
    C = np.tile(cs_head, (1, NH)).reshape(2, 128, 2 * 512)
    S = np.tile(ss_head, (1, NH)).reshape(2, 128, 2 * 512)
    cosx = np.concatenate([C[0], C[1]], axis=1).copy()
    sinx = np.concatenate([S[0], S[1]], axis=1).copy()
    pm1 = np.tile(np.concatenate([-np.ones(32, np.float32), np.ones(32, np.float32)]), NH)
    pm1 = np.tile(pm1[None, :], (128, 1)).astype(ml_dtypes.float16 if False else np.float16)

    i = np.arange(128)
    gA = (i[:, None] >= i[None, :]).astype(np.float32)
    gC = (i[:, None] <= i[None, :]).astype(np.float32)
    interior = np.concatenate([gA, gC], axis=1)
    first = np.concatenate([np.zeros_like(gA), gC], axis=1)
    last = np.concatenate([gA, np.zeros_like(gC)], axis=1)

    in_maps = []
    for core in range(8):
        b, h = core // 2, core % 2
        S0 = h * 2048
        lo, hi = S0 - 384, S0 + 2176
        lo_c, hi_c = max(lo, 0), min(hi, 4096)
        xs = np.zeros((TW, DM), np.float32)
        xs[lo_c - lo:hi_c - lo] = x[b, lo_c:hi_c]
        g1 = first if h == 0 else interior
        g2 = interior if h == 0 else last
        gates = np.stack([interior, g1, g2], axis=1).astype(ml_dtypes.bfloat16)
        bsel = np.full((128, 1), float(h), np.float32)
        in_maps.append(dict(
            x0=xs, wqkv=wqkv_p, wqkvb=wqkvb_p, wo=wo, wob=wob[None, :].copy(),
            wr=wr, wk=wk, wv=wv, mu=mu, cosx=cosx, sinx=sinx, pm1=pm1,
            gates=np.ascontiguousarray(gates), bsel=bsel))
    return in_maps


_NC_CACHE = None


def kernel(**inputs):
    global _NC_CACHE
    in_maps = host_prep(inputs)
    if _NC_CACHE is None:
        _NC_CACHE = build_nc()
    res = run_bass_kernel_spmd(_NC_CACHE, in_maps, core_ids=list(range(8)))
    out = np.empty((4, 4096, DM), np.float32)
    for core in range(8):
        b, h = core // 2, core % 2
        out[b, h * 2048:(h + 1) * 2048] = res.results[core]["out"]
    return out
